# version 07
# baseline (speedup 1.0000x reference)
import numpy as np
import ml_dtypes

import concourse.bass as bass
import concourse.mybir as mybir
from concourse.bass_isa import InstPseudoReloadLibraryIndex, ReduceOp
from concourse.bass_utils import run_bass_kernel_spmd

F32 = mybir.dt.float32
BF16 = mybir.dt.bfloat16
ALU = mybir.AluOpType
AF = mybir.ActivationFunctionType
AX = mybir.AxisListType

from concourse.dve_op import DveOp
from concourse.dve_spec import (Spec, Src0, Src1, C0, C1, C2, C3, Zero, lower,
                                select, sq)
import concourse.dve_ops as dve_ops


def _cdt(a):
    return np.float64 if a.dtype == np.float64 else np.float32


def _mk_dveop(name, body, reference, perf_en=True):
    spec = Spec(body=body)
    shas, pe = {}, {}
    for ver in ("v3", "v4"):
        try:
            c = lower(spec, ver=ver, perf_en=perf_en, name=name)
            shas[ver], pe[ver] = c.sha(ver), perf_en
        except Exception:
            c = lower(spec, ver=ver, perf_en=False, name=name)
            shas[ver], pe[ver] = c.sha(ver), False
    return DveOp(name, spec, subdim=False, uops_sha=shas, perf_en=pe,
                 reference=reference)


def _ref_fp4a(in0, in1, s0, s1, imm2):
    f = _cdt(in0)
    y2 = (in0.astype(f) * in0.astype(f)).astype(f)
    c3 = np.asarray(in1, dtype=f)
    return (((y2 > s0).astype(f) + (y2 > s1).astype(f))
            + ((y2 > imm2).astype(f) + (y2 > c3).astype(f))).astype(f)


def _ref_fp4b(in0, in1, s0, s1, imm2):
    f = _cdt(in0)
    y2 = (in0.astype(f) * in0.astype(f)).astype(f)
    t = (y2 > s0).astype(f) + (y2 > s1).astype(f)
    return ((t + t) + in1.astype(f)).astype(f)


def _ref_fp4c(in0, in1, s0, s1, imm2):
    f = _cdt(in0)
    y = in1.astype(f)
    y2 = (y * y).astype(f)
    d = (y2 > s0).astype(f)
    e = d + d
    g = in0.astype(f) + (e + e)
    return np.where(y > 0, g, -g).astype(f)


_y2 = sq(Src0)
FP4A = _mk_dveop("FP4LOW",
                 ((_y2 > C0) + (_y2 > C1)) + ((_y2 > C2) + (_y2 > C3)),
                 _ref_fp4a)
_t = (_y2 > C0) + (_y2 > C1)
FP4B = _mk_dveop("FP4MID", (_t + _t) + Src1, _ref_fp4b)
_y2b = sq(Src1)
_d = (_y2b > C0)
_e = _d + _d
_g = Src0 + (_e + _e)
FP4C = _mk_dveop("FP4SGN", select(Src1 > Zero, _g, Zero - _g), _ref_fp4c)

_REGISTERED = False


def _register_fp4():
    global _REGISTERED
    if _REGISTERED:
        return
    for op in (FP4A, FP4B, FP4C):
        if not any(o.name == op.name for o in dve_ops.OPS):
            dve_ops.OPS.append(op)
    if dve_ops._CUSTOM_DVE_ROW_BASE + len(dve_ops.OPS) - 1 >= 0x20:
        dve_ops.OPS[:] = [o for o in dve_ops.OPS
                          if not o.name.startswith("DVEHASH32L")]
    dve_ops._SUB_OPCODE_FOR_NAME.clear()
    dve_ops._SUB_OPCODE_FOR_NAME.update({
        op.name: dve_ops._CUSTOM_DVE_ROW_BASE + i
        for i, op in enumerate(dve_ops.OPS)})
    for op in (FP4A, FP4B, FP4C):
        dve_ops.CUSTOM_DVE_REFS[op.name] = op.ref
        try:
            dve_ops.CUSTOM_DVE_SPECS[op.name] = op.spec
        except Exception:
            pass
    _REGISTERED = True


ENGS = ("sync", "scalar", "vector", "tensor", "gpsimd")


class Plan:
    def __init__(self, nc):
        self.nc = nc
        self.ops = {e: [] for e in ENGS}
        self.sems = {}
        self.hiw = {}

    def sem(self, name):
        h = self.nc.semaphore(name).__enter__()
        self.sems[name] = [h, 0]

    def inc(self, s, k):
        self.sems[s][1] += k
        return self.sems[s][1]

    def val(self, s):
        return self.sems[s][1]

    def add(self, eng, fn):
        self.ops[eng].append(fn)

    def wait(self, eng, s, tgt):
        if tgt is None or tgt <= 0:
            return
        key = (eng, s)
        if self.hiw.get(key, -1) >= tgt:
            return
        self.hiw[key] = tgt
        h = self.sems[s][0]
        self.add(eng, lambda e: e.wait_ge(h, tgt))

    def dma(self, eng, s, out, in_, transpose=False):
        h = self.sems[s][0]
        tgt = self.inc(s, 16)
        if transpose:
            self.add(eng,
                     lambda e: e.dma_start_transpose(out, in_).then_inc(h, 16))
        else:
            self.add(eng,
                     lambda e: e.dma_start(out=out, in_=in_).then_inc(h, 16))
        return tgt

    def run(self, eng, fn, s=None, k=1):
        if s is None:
            self.add(eng, fn)
            return None
        h = self.sems[s][0]
        tgt = self.inc(s, k)
        self.add(eng, lambda e: fn(e).then_inc(h, k))
        return tgt

    def drain(self, eng):
        self.add(eng, lambda e: e.drain())

    def waitp(self, eng, pair):
        if pair:
            self.wait(eng, pair[0], pair[1])

    def row(self, r):
        if getattr(self, "_pe_row", "F") != r:
            for _ in range(16):
                self.add("tensor", lambda e: e.nop())
            self._pe_row = r

    def marker(self, eng, s):
        h = self.sems[s][0]
        tgt = self.inc(s, 1)
        self.add(eng, lambda e: e.drain())
        self.add(eng, lambda e: e.sem_inc(h, 1))
        return tgt


D = 1024
H, HD = 16, 64
LH = 8
GD = 512
I = 2816
NBI = I // 16
TILE = 512


def build_core(nseq=2, seqlen=4096):
    import os
    ABL = int(os.environ.get("K_ABL", "0"))
    _register_fp4()
    T = nseq * seqlen
    NT = T // TILE
    TPS = seqlen // TILE
    NC_TOK = T // 128
    LATB = 64 * (seqlen // 4096) if seqlen >= 4096 else max(seqlen // 64, 8)
    LATB = seqlen // 64

    nc = bass.Bass("TRN2", target_bir_lowering=False)

    x_h = nc.dram_tensor("x", [T, D], F32, kind="ExternalInput").ap()
    wq_h = nc.dram_tensor("wq", [D, D], F32, kind="ExternalInput").ap()
    wk_h = nc.dram_tensor("wk", [D, D], F32, kind="ExternalInput").ap()
    wv_h = nc.dram_tensor("wv", [D, D], F32, kind="ExternalInput").ap()
    wgq_h = nc.dram_tensor("wgq", [D, GD], F32, kind="ExternalInput").ap()
    wgk_h = nc.dram_tensor("wgk", [D, GD], F32, kind="ExternalInput").ap()
    wgv_h = nc.dram_tensor("wgv", [D, GD], F32, kind="ExternalInput").ap()
    rout_h = nc.dram_tensor("rout", [GD, D], F32, kind="ExternalInput").ap()
    glw_h = nc.dram_tensor("glw", [H, HD], F32, kind="ExternalInput").ap()
    grw_h = nc.dram_tensor("grw", [H, HD], F32, kind="ExternalInput").ap()
    wout_h = nc.dram_tensor("wout", [D, D], F32, kind="ExternalInput").ap()
    wg_h = nc.dram_tensor("wgate", [I, D], F32, kind="ExternalInput").ap()
    wu_h = nc.dram_tensor("wup", [I, D], F32, kind="ExternalInput").ap()
    wd_h = nc.dram_tensor("wdown", [D, I], F32, kind="ExternalInput").ap()
    y_h = nc.dram_tensor("y", [T, D], F32, kind="ExternalOutput").ap()
    _spk = "ExternalOutput" if int(os.environ.get("K_DBG", "0")) else "Internal"
    xnsp = nc.dram_tensor("xnsp", [T, D], BF16, kind=_spk).ap()
    combsp = nc.dram_tensor("combsp", [T, D], BF16, kind=_spk).ap()
    xqsp = nc.dram_tensor("xqsp", [T, D], BF16, kind=_spk).ap()
    hqsp = nc.dram_tensor("hqsp", [T, I], BF16, kind=_spk).ap()
    wgnat = nc.dram_tensor("wgnat", [I, D], BF16, kind=_spk).ap()
    if _spk == "ExternalOutput":
        wgfmd = nc.dram_tensor("wgfmd", [128, 8 * I], BF16, kind=_spk).ap()
        xqfmd = nc.dram_tensor("xqfmd", [128, 8 * TILE], BF16, kind=_spk).ap()
    wunat = nc.dram_tensor("wunat", [I, D], BF16, kind=_spk).ap()
    wdnat = nc.dram_tensor("wdnat", [D, I], BF16, kind=_spk).ap()

    BASE = 16384 + 64
    TOP = 224 * 1024 - 64

    class Lay:
        def __init__(self, start):
            self.cur = start

        def take(self, name, shape, dtype, nbytes_pp):
            off = self.cur
            self.cur = (self.cur + nbytes_pp + 31) & ~31
            return nc.alloc_sbuf_tensor_at(name, shape, dtype, offset=off)

        def t(self, name, p, n, dtype):
            size = n * (4 if dtype == F32 else 2)
            return self.take(name, [p, n], dtype, size)

    st = Lay(BASE)
    iota_p = st.t("iota_p", 128, 1, F32)
    iota_p1 = st.t("iota_p1", 128, 1, F32)
    ends_r = st.t("ends_r", 128, TPS * 4, BF16)
    ident = st.t("ident", 128, 128, BF16)
    cend = st.t("cend", 128, 2, F32)
    c3c = st.t("c3c", 128, 1, F32)
    epsc = st.t("epsc", 128, 1, F32)
    csc = st.t("csc", 128, 40, F32)
    cacc = st.t("cacc", 128, 20, F32)
    UBASE = st.cur

    pb = Lay(UBASE)
    wq_s = pb.t("wq_s", 128, 8 * D, BF16)
    wk_s = pb.t("wk_s", 128, 8 * D, BF16)
    wv_s = pb.t("wv_s", 128, 8 * D, BF16)
    wgq_s = pb.t("wgq_s", 128, 8 * GD, BF16)
    wgk_s = pb.t("wgk_s", 128, 8 * GD, BF16)
    wgv_s = pb.t("wgv_s", 128, 8 * GD, BF16)
    rout_s = pb.t("rout_s", 128, 4 * D, BF16)
    gdiag_s = pb.t("gdiag_s", 128, 8 * 32, BF16)
    xnfm = pb.t("xnfm", 128, 8 * TILE, BF16)
    q_sb = pb.t("q_sb", 128, 8 * TILE, BF16)
    gq_sb = pb.t("gq_sb", 128, 4 * TILE, BF16)
    kring = pb.t("kring", 128, 8 * 1024, BF16)
    NVS = 6
    vring = pb.t("vring", 128, NVS * H * 65, BF16)
    p_sb = pb.t("p_sb", 128, 8 * LATB, BF16)
    p32 = pb.t("p32", 128, 8 * 8, F32)
    gk_sb = pb.t("gk_sb", 128, 4 * LATB, BF16)
    gv_sb = pb.t("gv_sb", 128, 4 * LATB, BF16)
    gvp_sb = pb.t("gvp_sb", 128, 4 * D, BF16)
    gates_pre = pb.t("gates_pre", 32, 512, BF16)
    gates_tm = pb.t("gates_tm", 128, 2 * 4 * 32, BF16)
    msk_sb = pb.t("msk_sb", 128, H * 384, BF16)
    pt_sb = pb.t("pt_sb", 128, 2 * H * 384, BF16)
    prem = pb.t("prem", 128, 2 * LH * 64, BF16)
    prt_sb = pb.t("prt_sb", 128, 2 * 512, BF16)
    rsum_r = pb.t("rsum_r", 128, 8, F32)
    c_l = pb.t("c_l", 128, 16, BF16)
    rem_sb = pb.t("rem_sb", 128, 2 * D, BF16)
    comb_sb = pb.t("comb_sb", 128, 2 * D, BF16)
    NXB = 2
    x_sb = pb.t("x_sb", 128, NXB * D, F32)
    bns = pb.t("bns", 128, NXB * 12, F32)
    mv = pb.t("mv", 128, NXB * 2, F32)
    lnst = pb.t("lnst", 128, NXB * 4, F32)
    xn_sb = pb.t("xn_sb", 128, D, BF16)
    xnf32 = pb.t("xnf32", 128, D, F32)
    qaB = pb.t("qaB", 128, D, BF16)
    xqB = pb.t("xqB", 128, D, BF16)
    qbmB = pb.t("qbmB", 128, 64, F32)
    qsB = pb.t("qsB", 128, 64, F32)
    qinvB = qbmB
    qhalfB = qsB
    assert pb.cur <= TOP, f"phase-B over: {pb.cur}"

    pc = Lay(UBASE)
    wgq_fm = pc.t("wgq_fm", 128, 8 * I, BF16)
    wuq_fm = pc.t("wuq_fm", 128, 8 * I, BF16)
    wdq_fm = pc.t("wdq_fm", 128, 22 * D, BF16)
    wout_s2 = pc.t("wout_s2", 128, 8 * D, BF16)
    hqfm = pc.t("hqfm", 128, 11 * TILE, BF16)
    xqfm2 = pc.t("xqfm2", 128, 8 * TILE, BF16)
    combfm = pc.t("combfm", 128, 8 * TILE, BF16)
    h_sb = pc.t("h_sb", 128, I, BF16)
    qy = pc.t("qy", 128, I, BF16)
    qa = pc.t("qa", 128, I, BF16)
    xn2_sb = pc.t("xn2_sb", 128, D, BF16)
    qbm = pc.t("qbm", 128, NBI, F32)
    qs = pc.t("qs", 128, NBI, F32)
    qinvb = pc.t("qinvb", 128, NBI, BF16)
    qhalf = pc.t("qhalf", 128, NBI, BF16)
    x2_sb = pc.t("x2_sb", 128, D, F32)
    out_sb = pc.t("out_sb", 128, D, F32)
    sg_sb = pc.t("sg_sb", 128, 512, BF16)
    assert pc.cur <= TOP, f"phase-C over: {pc.cur}"

    wp = Lay(UBASE)
    wqf = wp.t("wqf", 128, I, F32)
    wqy = wp.t("wqy", 128, I, F32)
    wqa = wp.t("wqa", 128, I, F32)
    wqout = wp.t("wqout", 128, I, BF16)
    wqbm = wp.t("wqbm", 128, NBI, F32)
    wqs = wp.t("wqs", 128, NBI, F32)
    wqinv = wp.t("wqinv", 128, NBI, F32)
    wqhalf = wp.t("wqhalf", 128, NBI, F32)
    assert wp.cur <= UBASE + 8 * I * 2, "wprep scratch exceeds wgq_fm span"

    arena = nc.alloc_psum_tensor("arena", [128, 4096], F32)

    def bank(b, w=512):
        return arena[:, b * 512:b * 512 + w]

    P = Plan(nc)
    for s in ("pe", "dv", "ac", "gp",
              "xld0", "xld1", "xnsp0", "xnsp1", "xfq0", "xfq1",
              "gtq0", "gtq1", "cb0", "cb1", "outst",
              "wfmq0", "wfmq1", "xqfq0", "xqfq1", "cbfq0", "cbfq1",
              "hqfq0", "hqfq1",
              "wat", "wpl", "wpst", "wo2", "xn2l", "xqsp", "hqsp", "x2l"):
        P.sem(s)

    P.run("gpsimd", lambda e: e.iota(iota_p[:], pattern=[[0, 1]], base=0,
                                     channel_multiplier=1,
                                     allow_small_or_imprecise_dtypes=True))
    P.run("gpsimd", lambda e: e.iota(iota_p1[:], pattern=[[0, 1]], base=128,
                                     channel_multiplier=1,
                                     allow_small_or_imprecise_dtypes=True))
    for t in range(TPS):
        for j in range(4):
            base = (t * 512 + j * 128) // 64 + 1
            P.run("gpsimd", lambda e, c=t * 4 + j, b=base: e.iota(
                ends_r[0:64, c:c + 1], pattern=[[0, 1]], base=b,
                channel_multiplier=0, allow_small_or_imprecise_dtypes=True))
            P.run("gpsimd", lambda e, c=t * 4 + j, b=base + 1: e.iota(
                ends_r[64:128, c:c + 1], pattern=[[0, 1]], base=b,
                channel_multiplier=0, allow_small_or_imprecise_dtypes=True))
    P.run("gpsimd", lambda e: e.iota(ident[:], pattern=[[1, 128]], base=0,
                                     channel_multiplier=-1,
                                     allow_small_or_imprecise_dtypes=True))
    P.add("gpsimd", lambda e: e.add_instruction(InstPseudoReloadLibraryIndex(
        name=nc.get_next_instruction_name(), ins=[], outs=[], lib_index=1)))
    gp_setup = P.marker("gpsimd", "gp")
    P.wait("vector", "gp", gp_setup)
    P.run("vector", lambda e: e.tensor_scalar(out=ident[:], in0=ident[:],
                                              scalar1=0.0, scalar2=None,
                                              op0=ALU.is_equal))
    P.run("vector", lambda e: e.memset(cend[:, 0:1], 128.0))
    P.run("vector", lambda e: e.memset(cend[:, 1:2], 256.0))
    P.run("vector", lambda e: e.memset(c3c[:], 3.0625))
    P.run("vector", lambda e: e.memset(epsc[:], 1e-5))
    P.run("vector", lambda e: e.memset(
        vring[:].rearrange("p (s c) -> p s c", c=65)[:, :, 64:65], 1.0))
    P.run("vector", lambda e: e.memset(p_sb[:], 0.0))
    P.run("vector", lambda e: e.memset(gk_sb[:], 0.0))
    P.run("vector", lambda e: e.memset(gv_sb[:], 0.0))
    P.run("vector", lambda e: e.memset(gdiag_s[:], 0.0))
    P.run("vector", lambda e: e.memset(prem[:], 0.0))
    P.run("vector", lambda e: e.memset(gvp_sb[:], 0.0))
    dv_setup = P.marker("vector", "dv")

    def wload(dst, src, n_out, nch):
        for c in range(nch):
            P.dma("gpsimd", "wat", dst[:, c * n_out:(c + 1) * n_out],
                  src[c * 128:(c + 1) * 128, :])

    wload(wq_s, wq_h, D, 8)
    wload(wk_s, wk_h, D, 8)
    wload(wv_s, wv_h, D, 8)
    wload(wgq_s, wgq_h, GD, 8)
    wload(wgk_s, wgk_h, GD, 8)
    wload(wgv_s, wgv_h, GD, 8)
    wload(rout_s, rout_h, D, 4)
    P.wait("gpsimd", "dv", dv_setup)
    for h in range(H):
        c = h // 2
        r0 = (h % 2) * 64
        P.dma("gpsimd", "wat", gdiag_s[r0:r0 + 64, c * 32 + h:c * 32 + h + 1],
              glw_h[h, :].unsqueeze(1))
        P.dma("gpsimd", "wat",
              gdiag_s[r0:r0 + 64, c * 32 + 16 + h:c * 32 + 17 + h],
              grw_h[h, :].unsqueeze(1))
    w_attn_loaded = ("wat", P.val("wat"))

    vslot = lambda ch: ch % NVS
    xn_spilled = [None] * NC_TOK
    xq_spilled = [0] * NC_TOK
    comb_stored = [None] * NC_TOK
    proj_pe_end = [0] * NT
    att_dv_end = [0] * NT
    prev_comb_ev = [0]
    prev_mask_dv = [0]
    mask_dv_bp = [0, 0]
    prev_m1 = [0]
    prev_m2 = [0]
    prev_m3 = [0]
    prev_exp = [0]
    prev_gt_sq = [0]
    prev_pv = [0, 0]
    prev_prtrans = [0, 0]
    prev_pool = [0]

    for t in range(NT):
        par = t % 2
        sq_i = t % TPS
        P.wait("tensor", "dv", prev_m3[0])
        if sq_i == 0 and t > 0:
            P.wait("vector", "pe", P.val("pe"))
            P.run("vector", lambda e: e.memset(p_sb[:], 0.0))
            P.run("vector", lambda e: e.memset(gk_sb[:], 0.0))
            P.run("vector", lambda e: e.memset(gv_sb[:], 0.0))
            P.drain("vector")

        for j in range(4):
            ch = t * 4 + j
            sl = ch % NXB
            row0 = ch * 128
            if ch >= NXB:
                P.wait("scalar", "xqsp", xq_spilled[ch - NXB])
            ld_x = P.dma("scalar", f"xld{sl}", x_sb[:, sl * D:(sl + 1) * D],
                         x_h[row0:row0 + 128, :])
            P.wait("vector", f"xld{sl}", ld_x)
            P.wait("vector", "dv", dv_setup)
            P.run("vector", lambda e, sl=sl: e.bn_stats(
                bns[:, sl * 12:sl * 12 + 6], x_sb[:, sl * D:sl * D + 512]))
            P.run("vector", lambda e, sl=sl: e.bn_stats(
                bns[:, sl * 12 + 6:sl * 12 + 12],
                x_sb[:, sl * D + 512:sl * D + 1024]))
            P.drain("vector")
            P.run("vector", lambda e, sl=sl: e.bn_aggr(
                mv[:, sl * 2:sl * 2 + 2], bns[:, sl * 12:sl * 12 + 12]))
            dv_mv = P.marker("vector", "dv")
            P.wait("scalar", "dv", dv_mv)
            P.run("scalar", lambda e, sl=sl: e.activation(
                out=lnst[:, sl * 4:sl * 4 + 1],
                in_=mv[:, sl * 2 + 1:sl * 2 + 2],
                func=AF.Sqrt, bias=epsc[:], scale=1.0))
            ac_sq = P.marker("scalar", "ac")
            P.wait("vector", "ac", ac_sq)
            P.run("vector", lambda e, sl=sl: e.reciprocal(
                out=lnst[:, sl * 4 + 1:sl * 4 + 2],
                in_=lnst[:, sl * 4:sl * 4 + 1]))
            P.drain("vector")
            P.run("vector", lambda e, sl=sl: e.scalar_tensor_tensor(
                out=lnst[:, sl * 4 + 2:sl * 4 + 3],
                in0=mv[:, sl * 2:sl * 2 + 1], scalar=-1.0,
                in1=lnst[:, sl * 4 + 1:sl * 4 + 2],
                op0=ALU.mult, op1=ALU.mult))
            dv_ln = P.marker("vector", "dv")
            P.wait("scalar", "dv", dv_ln)
            if ch >= 1:
                P.waitp("scalar", xn_spilled[ch - 1])
            ac_xn = P.run("scalar", lambda e, sl=sl: e.activation(
                out=xnf32[:, 0:D],
                in_=x_sb[:, sl * D:(sl + 1) * D],
                func=AF.Identity, bias=lnst[:, sl * 4 + 2:sl * 4 + 3],
                scale=lnst[:, sl * 4 + 1:sl * 4 + 2]), s="ac")
            P.wait("vector", "ac", ac_xn)
            if ch >= 1:
                P.waitp("vector", xn_spilled[ch - 1])
            xnb_done = P.run("vector", lambda e: e.tensor_copy(
                out=xn_sb[:, 0:D], in_=xnf32[:, 0:D]), s="dv")
            P.drain("vector")
            P.wait("scalar", "dv", xnb_done)
            xn_spilled[ch] = ("xnsp0",
                              P.dma("scalar", "xnsp0",
                                    xnsp[row0:row0 + 128, :],
                                    xn_sb[:, 0:D]))
            P.run("vector", lambda e: e.tensor_reduce(
                out=qbmB[:, 0:64],
                in_=xnf32[:, 0:D].rearrange("p (b k) -> p b k", k=16),
                axis=AX.X, op=ALU.max, apply_absolute_value=True))
            P.drain("vector")
            P.run("vector", lambda e: e.tensor_scalar(
                out=qsB[:, 0:64], in0=qbmB[:, 0:64],
                scalar1=float(np.float32(1.0 / 6.0)), scalar2=1e-12,
                op0=ALU.mult, op1=ALU.max))
            P.drain("vector")
            P.run("vector", lambda e: e.reciprocal(
                out=qinvB[:, 0:64], in_=qsB[:, 0:64]))
            P.drain("vector")
            P.run("vector", lambda e: e.tensor_scalar(
                out=qhalfB[:, 0:64], in0=qsB[:, 0:64], scalar1=0.5,
                scalar2=None, op0=ALU.mult))
            P.drain("vector")
            P.run("vector", lambda e, sl=sl: e.tensor_tensor(
                out=x_sb[:, sl * D:(sl + 1) * D]
                    .rearrange("p (b k) -> p b k", k=16),
                in0=xnf32[:, 0:D].rearrange("p (b k) -> p b k", k=16),
                in1=qinvB[:, 0:64].unsqueeze(2).broadcast_to([128, 64, 16]),
                op=ALU.mult))
            P.drain("vector")
            P.run("vector", lambda e, sl=sl: e._custom_dve(
                FP4A, out=qaB[:, 0:D], in0=x_sb[:, sl * D:(sl + 1) * D],
                in1=c3c[:], s0=0.0625, s1=0.5625, imm2=1.5625))
            P.drain("vector")
            P.run("vector", lambda e, sl=sl: e._custom_dve(
                FP4B, out=qaB[:, 0:D], in0=x_sb[:, sl * D:(sl + 1) * D],
                in1=qaB[:, 0:D], s0=6.25, s1=12.25))
            P.drain("vector")
            P.run("vector", lambda e, sl=sl: e._custom_dve(
                FP4C, out=qaB[:, 0:D], in0=qaB[:, 0:D],
                in1=x_sb[:, sl * D:(sl + 1) * D], s0=25.0))
            P.drain("vector")
            if ch >= 1:
                P.wait("vector", "xqsp", xq_spilled[ch - 1])
            xqB_done = P.run("vector", lambda e: e.tensor_tensor(
                out=xqB[:, 0:D].rearrange("p (b k) -> p b k", k=16),
                in0=qaB[:, 0:D].rearrange("p (b k) -> p b k", k=16),
                in1=qhalfB[:, 0:64].unsqueeze(2).broadcast_to([128, 64, 16]),
                op=ALU.mult), s="dv")
            P.drain("vector")
            P.wait("scalar", "dv", xqB_done)
            xq_spilled[ch] = P.dma("scalar", "xqsp", xqsp[row0:row0 + 128, :],
                                   xqB[:, 0:D])

        if t >= 1:
            P.wait("sync", "pe", proj_pe_end[t - 1])
            P.wait("sync", "dv", prev_pool[0])
        xl_sq = 0
        for c in range(8):
            P.waitp("sync", xn_spilled[t * 4 + 3])
            xl_sq = P.dma("sync", "xfq0", xnfm[:, c * TILE:(c + 1) * TILE],
                          xnsp[t * TILE:(t + 1) * TILE,
                               c * 128:(c + 1) * 128],
                          transpose=True)

        if t == 0:
            P.waitp("tensor", w_attn_loaded)
        P.wait("tensor", "xfq0", xl_sq)
        P.wait("tensor", "dv", prev_comb_ev[0])

        def proj_ws(wsb, n_out, noc, dst, dst_scale, evac_eng, evac_sem,
                    dstc0=0, dst_stride=TILE):
            P.row("F")
            P.wait("tensor", "ac", P.val("ac"))
            P.wait("tensor", "dv", P.val("dv"))
            tails = []
            for oc in range(noc):
                bk = oc % 2
                if len(tails) >= 2:
                    P.wait("tensor", evac_sem, tails[-2])
                for k in range(8):
                    P.run("tensor", lambda e, oc=oc, k=k, bk=bk: e.matmul(
                        bank(bk),
                        wsb[:, k * n_out + oc * 128:k * n_out + oc * 128 + 128],
                        xnfm[:, k * TILE:(k + 1) * TILE],
                        start=(k == 0), stop=(k == 7)),
                        s="pe" if k == 7 else None)
                pe_t = P.val("pe")
                P.wait(evac_eng, "pe", pe_t)
                c0 = dstc0 + oc * dst_stride
                if evac_eng == "scalar":
                    tails.append(P.run("scalar", lambda e, c0=c0, bk=bk:
                        e.activation(out=dst[:, c0:c0 + TILE], in_=bank(bk),
                                     func=AF.Copy, bias=0.0, scale=dst_scale),
                        s=evac_sem))
                else:
                    tails.append(P.run("vector", lambda e, c0=c0, bk=bk:
                        e.tensor_copy(out=dst[:, c0:c0 + TILE], in_=bank(bk)),
                        s=evac_sem))
            return tails[-1]

        q_done = proj_ws(wq_s, D, 8, q_sb, 0.25, "scalar", "ac")
        k_done = proj_ws(wk_s, D, 8, kring, 1.0, "vector", "dv",
                         dstc0=par * 512, dst_stride=1024)

        v_done = 0
        P.row("F")
        for j in range(4):
            ch = t * 4 + j
            sl = vslot(ch)
            for s in range(2):
                for k in range(8):
                    P.run("tensor", lambda e, j=j, s=s, k=k: e.matmul(
                        bank(2 + s),
                        xnfm[:, k * TILE + j * 128:k * TILE + (j + 1) * 128],
                        wv_s[:, k * D + s * 512:k * D + (s + 1) * 512],
                        start=(k == 0), stop=(k == 7)),
                        s="pe" if k == 7 else None)
            pe_v = P.val("pe")
            P.wait("vector", "pe", pe_v)
            v_done = P.run("vector", lambda e, sl=sl: e.tensor_copy(
                out=vring[:, sl * H * 65:(sl + 1) * H * 65]
                    .rearrange("p (h c) -> p h c", c=65)[:, :, 0:64],
                in_=bank(2, 1024).rearrange("p (h c) -> p h c", c=64)),
                s="dv")
            P.wait("tensor", "dv", v_done)
        gq_done = proj_ws(wgq_s, GD, 4, gq_sb, 0.25, "scalar", "ac")
        proj_pe_end[t] = P.val("pe")

        P.wait("vector", "xfq0", xl_sq)
        pool_done = 0
        for c in range(8):
            P.run("vector", lambda e, c=c: e.tensor_reduce(
                out=p32[:, c * 8:(c + 1) * 8],
                in_=xnfm[:, c * TILE:(c + 1) * TILE]
                    .rearrange("p (l k) -> p l k", k=64),
                axis=AX.X, op=ALU.add))
        P.drain("vector")
        pool_done = P.run("vector", lambda e, sq_i=sq_i: e.tensor_copy(
            out=p_sb[:].rearrange("p (c l) -> p c l", l=LATB)
                [:, :, sq_i * 8:sq_i * 8 + 8],
            in_=p32[:].rearrange("p (c l) -> p c l", l=8)), s="dv")
        P.drain("vector")
        prev_pool[0] = pool_done

        P.wait("tensor", "dv", pool_done)
        gkgv_done = 0
        P.row("F")
        for (wsb, dst) in ((wgk_s, gk_sb), (wgv_s, gv_sb)):
            for oc in range(4):
                for k in range(8):
                    P.run("tensor", lambda e, wsb=wsb, oc=oc, k=k,
                          sq_i=sq_i: e.matmul(
                        bank(6, 8),
                        wsb[:, k * GD + oc * 128:k * GD + oc * 128 + 128],
                        p_sb[:, k * LATB + sq_i * 8:k * LATB + sq_i * 8 + 8],
                        start=(k == 0), stop=(k == 7)),
                        s="pe" if k == 7 else None)
                pe_g = P.val("pe")
                P.wait("scalar", "pe", pe_g)
                gkgv_done = P.run("scalar", lambda e, oc=oc, dst=dst,
                                  sq_i=sq_i:
                    e.activation(
                        out=dst[:, oc * LATB + sq_i * 8:oc * LATB + sq_i * 8 + 8],
                        in_=bank(6, 8), func=AF.Copy, bias=0.0,
                        scale=1.0 / 64.0), s="ac")
                P.wait("tensor", "ac", gkgv_done)

        gvp_done = 0
        LW = min(LATB, 64)
        for pb_i in range(4):
            for hh in range(2):
                P.row(hh * 64)
                for s in range(2):
                    P.run("tensor", lambda e, pb_i=pb_i, s=s, hh=hh,
                          LW=LW: e.matmul(
                        bank(6 + s)[hh * 64:hh * 64 + LW, :],
                        gv_sb[hh * 64:(hh + 1) * 64,
                              pb_i * LATB:pb_i * LATB + LW],
                        rout_s[hh * 64:(hh + 1) * 64,
                               pb_i * D + s * 512:pb_i * D + (s + 1) * 512],
                        start=True, stop=True,
                        tile_position=(hh * 64, hh * 64)),
                        s="pe" if (s == 1 and hh == 1) else None)
            pe_gvp = P.val("pe")
            P.wait("vector", "pe", pe_gvp)
            if LW == 64:
                gvp_done = P.run("vector", lambda e, pb_i=pb_i: e.tensor_copy(
                    out=gvp_sb[:, pb_i * D:(pb_i + 1) * D], in_=bank(6, 1024)),
                    s="dv")
            else:
                for hh in range(2):
                    gvp_done = P.run("vector",
                        lambda e, pb_i=pb_i, hh=hh, LW=LW: e.tensor_copy(
                            out=gvp_sb[hh * 64:hh * 64 + LW,
                                       pb_i * D:(pb_i + 1) * D],
                            in_=bank(6, 1024)[hh * 64:hh * 64 + LW, :]),
                        s="dv")
                P.drain("vector")
            P.wait("tensor", "dv", gvp_done)

        P.row("F")
        for k in range(8):
            P.run("tensor", lambda e, k=k: e.matmul(
                bank(6)[0:32, :], gdiag_s[:, k * 32:(k + 1) * 32],
                q_sb[:, k * TILE:(k + 1) * TILE],
                start=(k == 0), stop=(k == 7)), s="pe" if k == 7 else None)
        pe_gt = P.val("pe")
        P.wait("scalar", "pe", pe_gt)
        P.wait("scalar", "gtq0", prev_gt_sq[0])
        g_sig = P.run("scalar", lambda e: e.activation(
            out=gates_pre[:], in_=bank(6)[0:32, :],
            func=AF.Sigmoid, bias=0.0, scale=4.0), s="ac")
        gt_sq = 0
        for jj in range(4):
            P.wait("sync", "ac", g_sig)
            if t >= 2:
                P.wait("sync", "dv", att_dv_end[t - 2])
            gt_sq = P.dma("sync", "gtq0",
                          gates_tm[:, par * 128 + jj * 32:
                                   par * 128 + (jj + 1) * 32],
                          gates_pre[0:32, jj * 128:(jj + 1) * 128],
                          transpose=True)
        prev_gt_sq[0] = gt_sq

        if ABL == 1:
            for jj in range(4):
                qi = t * 4 + jj
                qp = qi % 2
                P.wait("vector", "pe", P.val("pe"))
                if qi >= 2:
                    P.waitp("vector", comb_stored[qi - 2])
                comb_ev = P.run("vector", lambda e, qp=qp: e.memset(
                    comb_sb[:, qp * D:(qp + 1) * D], 0.0), s="dv")
                P.drain("vector")
                prev_comb_ev[0] = comb_ev
                P.wait("scalar", "dv", comb_ev)
                comb_stored[qi] = (f"cb{qp}",
                                   P.dma("scalar", f"cb{qp}",
                                         combsp[qi * 128:(qi + 1) * 128, :],
                                         comb_sb[:, qp * D:(qp + 1) * D]))
            att_dv_end[t] = P.val("dv")
            continue
        for jj in range(4):
            qi = t * 4 + jj
            sqj = sq_i * 4 + jj
            qp = qi % 2
            self_ch = qi
            if sqj == 0:
                kcs = [(self_ch, 0)]
                wid = 128
            elif sqj == 1:
                kcs = [(self_ch, 0), (self_ch - 1, 1)]
                wid = 256
            else:
                kcs = [(self_ch - 2, 0), (self_ch, 1), (self_ch - 1, 2)]
                wid = 384

            mask_dv_bp[0] = max(mask_dv_bp[0], prev_m1[0])
            mask_dv_bp[1] = max(mask_dv_bp[1], prev_m2[0])
            for p_i in range(8):
                hA = 2 * p_i
                bp = 4 * (p_i % 2)
                P.wait("tensor", "dv", mask_dv_bp[p_i % 2])
                for hh in range(2):
                    h = hA + hh
                    P.row((h % 2) * 64)
                    for ci, (chk, role) in enumerate(kcs):
                        half = ((chk // 4) % 2) * 512
                        koff = (chk % 4) * 128
                        P.run("tensor", lambda e, chk=chk, role=role, hh=hh,
                              h=h, half=half, koff=koff, ci=ci, jj=jj, bp=bp,
                              nk=len(kcs): e.matmul(
                            bank(bp + hh)[:, role * 128:(role + 1) * 128],
                            kring[(h % 2) * 64:(h % 2) * 64 + 64,
                                  (h // 2) * 1024 + half + koff:
                                  (h // 2) * 1024 + half + koff + 128],
                            q_sb[(h % 2) * 64:(h % 2) * 64 + 64,
                                 (h // 2) * TILE + jj * 128:
                                 (h // 2) * TILE + (jj + 1) * 128],
                            start=(ci == 0), stop=(ci == nk - 1),
                            tile_position=((h % 2) * 64, 0)),
                            s="pe" if (ci == len(kcs) - 1 and hh == 1) else None)
                pe_sc = P.val("pe")
                P.wait("vector", "pe", pe_sc)
                if p_i == 0:
                    P.wait("vector", "ac", prev_exp[0])
                for hh in range(2):
                    h = hA + hh
                    if sqj == 0:
                        stt, end = iota_p[:], cend[:, 0:1]
                    elif sqj == 1:
                        stt, end = iota_p[:], cend[:, 1:2]
                    else:
                        stt, end = iota_p1[:], iota_p[:]
                    P.run("vector",
                        lambda e, hh=hh, h=h, stt=stt, end=end, bp=bp,
                               w=wid: e.tensor_mask_reduce(
                            out=msk_sb[:, h * 384:h * 384 + w],
                            in_=bank(bp + hh)[:, 0:w],
                            mask_start=stt, mask_end=end, scale=0.5,
                            accum_in=-1e30,
                            op=ALU.max, accum_out=cacc[:, 4 + h:5 + h]),
                        s="dv")
                mask_dv_bp[p_i % 2] = P.val("dv")
            P.drain("vector")
            P.run("vector", lambda e: e.tensor_reduce(
                out=cacc[:, 0:1], in_=cacc[:, 4:20],
                axis=AX.X, op=ALU.max, apply_absolute_value=False), s="dv")
            P.drain("vector")
            masks_done = P.val("dv")
            P.wait("gpsimd", "dv", masks_done)
            gp_c = P.run("gpsimd", lambda e: e.partition_all_reduce(
                cacc[:, 1:2], cacc[:, 0:1], channels=128,
                reduce_op=ReduceOp.max), s="gp")
            P.wait("vector", "gp", gp_c)
            P.run("vector", lambda e: e.tensor_scalar(
                out=cacc[:, 2:3], in0=cacc[:, 1:2], scalar1=-1.0,
                scalar2=None, op0=ALU.mult))
            negc_done = P.marker("vector", "dv")

            P.wait("scalar", "dv", negc_done)
            P.wait("scalar", "pe", prev_pv[qp])
            exp_done = 0
            for h in range(H):
                exp_done = P.run("scalar", lambda e, h=h, w=wid,
                                 qp=qp: e.activation(
                    out=pt_sb[:, qp * H * 384 + h * 384:
                              qp * H * 384 + h * 384 + w],
                    in_=msk_sb[:, h * 384:h * 384 + w],
                    func=AF.Exp, bias=cacc[:, 2:3], scale=1.0), s="ac")
            prev_exp[0] = exp_done

            if ABL == 3:
                P.wait("vector", "ac", exp_done)
                P.wait("vector", "pe", P.val("pe"))
                if qi >= 2:
                    P.waitp("vector", comb_stored[qi - 2])
                comb_ev = P.run("vector", lambda e, qp=qp: e.memset(
                    comb_sb[:, qp * D:(qp + 1) * D], 0.0), s="dv")
                P.drain("vector")
                prev_comb_ev[0] = comb_ev
                prev_pv[qp] = P.val("pe")
                P.wait("scalar", "dv", comb_ev)
                comb_stored[qi] = (f"cb{qp}",
                                   P.dma("scalar", f"cb{qp}",
                                         combsp[qi * 128:(qi + 1) * 128, :],
                                         comb_sb[:, qp * D:(qp + 1) * D]))
                continue
            P.wait("tensor", "ac", exp_done)
            P.wait("tensor", "dv", v_done)
            nkc = len(kcs)
            pv_done = 0
            P.row("F")
            for h in range(H):
                for ci, (chk, role) in enumerate(kcs):
                    pv_done = P.run("tensor",
                        lambda e, h=h, chk=chk, role=role, ci=ci, qp=qp,
                               nkc=nkc: e.matmul(
                            bank(2 + h // 4)[:, (h % 4) * 65:(h % 4) * 65 + 65],
                            pt_sb[:, qp * H * 384 + h * 384 + role * 128:
                                  qp * H * 384 + h * 384 + (role + 1) * 128],
                            vring[:, vslot(chk) * H * 65 + h * 65:
                                  vslot(chk) * H * 65 + (h + 1) * 65],
                            start=(h % 4 == 0 and ci == 0),
                            stop=(h % 4 == 3 and ci == nkc - 1)),
                        s="pe" if (h == H - 1 and ci == nkc - 1) else None)
            prev_pv[qp] = pv_done

            if ABL == 4:
                P.wait("vector", "pe", pv_done)
                if qi >= 2:
                    P.waitp("vector", comb_stored[qi - 2])
                comb_ev = P.run("vector", lambda e, qp=qp: e.memset(
                    comb_sb[:, qp * D:(qp + 1) * D], 0.0), s="dv")
                P.drain("vector")
                prev_comb_ev[0] = comb_ev
                P.wait("scalar", "dv", comb_ev)
                comb_stored[qi] = (f"cb{qp}",
                                   P.dma("scalar", f"cb{qp}",
                                         combsp[qi * 128:(qi + 1) * 128, :],
                                         comb_sb[:, qp * D:(qp + 1) * D]))
                continue
            P.wait("tensor", "ac", g_sig)
            P.wait("tensor", "dv", prev_m3[0])
            for hh in range(2):
                P.row(hh * 64)
                for ci, h in enumerate((hh, 2 + hh, 4 + hh, 6 + hh)):
                    P.run("tensor", lambda e, h=h, jj=jj, ci=ci,
                          hh=hh: e.matmul(
                        bank(6 + hh)[:, (h // 2) * 64:(h // 2) * 64 + LATB],
                        gq_sb[hh * 64:hh * 64 + 64,
                              (h // 2) * TILE + jj * 128:
                              (h // 2) * TILE + (jj + 1) * 128],
                        gk_sb[hh * 64:hh * 64 + 64,
                              (h // 2) * LATB:(h // 2) * LATB + LATB],
                        start=(ci == 0), stop=(ci == 3),
                        tile_position=(hh * 64, 0)),
                        s="pe" if (hh == 1 and ci == 3) else None)
            pe_rs = P.val("pe")
            if ABL in (9, 10):
                P.wait("vector", "pe", pe_rs)
                if qi >= 2:
                    P.waitp("vector", comb_stored[qi - 2])
                comb_ev = P.run("vector", lambda e, qp=qp: e.memset(
                    comb_sb[:, qp * D:(qp + 1) * D], 0.0), s="dv")
                P.drain("vector")
                prev_comb_ev[0] = comb_ev
                P.wait("scalar", "dv", comb_ev)
                comb_stored[qi] = (f"cb{qp}",
                                   P.dma("scalar", f"cb{qp}",
                                         combsp[qi * 128:(qi + 1) * 128, :],
                                         comb_sb[:, qp * D:(qp + 1) * D]))
                continue
            P.wait("vector", "pe", pe_rs)
            for h in range(LH):
                P.run("vector", lambda e, h=h, sq_i=sq_i,
                      jj=jj: e.tensor_mask_reduce(
                    out=bank(6 + h % 2)[:, (h // 2) * 64:(h // 2) * 64 + LATB],
                    in_=bank(6 + h % 2)[:, (h // 2) * 64:(h // 2) * 64 + LATB],
                    mask_start=0.0,
                    mask_end=ends_r[:, sq_i * 4 + jj:sq_i * 4 + jj + 1],
                    scale=0.5, accum_in=-1e30,
                    op=ALU.max, accum_out=cacc[:, 4 + h:5 + h]), s="dv")
            P.drain("vector")
            rmask_done = P.run("vector", lambda e: e.tensor_reduce(
                out=cacc[:, 0:1], in_=cacc[:, 4:12],
                axis=AX.X, op=ALU.max, apply_absolute_value=False), s="dv")
            P.drain("vector")
            if ABL == 8:
                P.wait("vector", "dv", rmask_done)
                if qi >= 2:
                    P.waitp("vector", comb_stored[qi - 2])
                comb_ev = P.run("vector", lambda e, qp=qp: e.memset(
                    comb_sb[:, qp * D:(qp + 1) * D], 0.0), s="dv")
                P.drain("vector")
                prev_comb_ev[0] = comb_ev
                P.wait("scalar", "dv", comb_ev)
                comb_stored[qi] = (f"cb{qp}",
                                   P.dma("scalar", f"cb{qp}",
                                         combsp[qi * 128:(qi + 1) * 128, :],
                                         comb_sb[:, qp * D:(qp + 1) * D]))
                continue
            P.wait("gpsimd", "dv", rmask_done)
            gp_cr = P.run("gpsimd", lambda e: e.partition_all_reduce(
                cacc[:, 1:2], cacc[:, 0:1], channels=128,
                reduce_op=ReduceOp.max), s="gp")
            P.wait("vector", "gp", gp_cr)
            P.run("vector", lambda e: e.tensor_scalar(
                out=cacc[:, 3:4], in0=cacc[:, 1:2], scalar1=-1.0,
                scalar2=None, op0=ALU.mult))
            negcr_done = P.marker("vector", "dv")
            P.wait("scalar", "dv", negcr_done)
            P.wait("scalar", "pe", prev_prtrans[qp])
            rexp_done = 0
            for h in range(LH):
                rexp_done = P.run("scalar", lambda e, h=h, qp=qp: e.activation(
                    out=prem[:, qp * LH * 64 + h * 64:
                             qp * LH * 64 + h * 64 + LATB],
                    in_=bank(6 + h % 2)[:, (h // 2) * 64:
                                        (h // 2) * 64 + LATB],
                    func=AF.Exp, bias=cacc[:, 3:4], scale=1.0,
                    accum_out=rsum_r[:, h:h + 1]), s="ac")
            P.wait("vector", "ac", rexp_done)
            P.run("vector", lambda e: e.reciprocal(
                out=csc[:, 8:16], in_=rsum_r[:, 0:8]))
            P.drain("vector")
            prn_done = 0
            for h in range(LH):
                prn_done = P.run("vector", lambda e, h=h, qp=qp: e.tensor_scalar(
                    out=prem[:, qp * LH * 64 + h * 64:
                             qp * LH * 64 + h * 64 + LATB],
                    in0=prem[:, qp * LH * 64 + h * 64:
                             qp * LH * 64 + h * 64 + LATB],
                    scalar1=csc[:, 8 + h:9 + h], scalar2=None, op0=ALU.mult),
                    s="dv")
                P.drain("vector")
            if ABL == 6:
                P.wait("vector", "dv", prn_done)
                if qi >= 2:
                    P.waitp("vector", comb_stored[qi - 2])
                comb_ev = P.run("vector", lambda e, qp=qp: e.memset(
                    comb_sb[:, qp * D:(qp + 1) * D], 0.0), s="dv")
                P.drain("vector")
                prev_comb_ev[0] = comb_ev
                prev_prtrans[qp] = P.val("pe")
                P.wait("scalar", "dv", comb_ev)
                comb_stored[qi] = (f"cb{qp}",
                                   P.dma("scalar", f"cb{qp}",
                                         combsp[qi * 128:(qi + 1) * 128, :],
                                         comb_sb[:, qp * D:(qp + 1) * D]))
                continue
            P.wait("tensor", "dv", prn_done)
            prt_done = 0
            prt_copies = [0, 0, 0, 0]
            for pb_i in range(4):
                tb = pb_i % 2
                if pb_i >= 2:
                    P.wait("tensor", "dv", prt_copies[pb_i - 2])
                P.row("F")
                tr = P.run("tensor", lambda e, pb_i=pb_i, qp=qp,
                           tb=tb: e.transpose(
                    bank(tb, 64).bitcast(BF16),
                    prem[:, qp * LH * 64 + pb_i * 128:
                         qp * LH * 64 + (pb_i + 1) * 128],
                    ident[:]), s="pe")
                P.wait("vector", "pe", tr)
                prt_done = P.run("vector", lambda e, pb_i=pb_i,
                                 qp=qp, tb=tb: e.tensor_copy(
                    out=prt_sb[:, qp * 512 + pb_i * 128:
                               qp * 512 + (pb_i + 1) * 128],
                    in_=bank(tb, 64).bitcast(BF16)), s="dv")
                prt_copies[pb_i] = prt_done
            P.wait("tensor", "dv", prt_done)
            prev_prtrans[qp] = P.val("pe")
            if ABL == 7:
                P.wait("vector", "dv", prt_done)
                if qi >= 2:
                    P.waitp("vector", comb_stored[qi - 2])
                comb_ev = P.run("vector", lambda e, qp=qp: e.memset(
                    comb_sb[:, qp * D:(qp + 1) * D], 0.0), s="dv")
                P.drain("vector")
                prev_comb_ev[0] = comb_ev
                P.wait("scalar", "dv", comb_ev)
                comb_stored[qi] = (f"cb{qp}",
                                   P.dma("scalar", f"cb{qp}",
                                         combsp[qi * 128:(qi + 1) * 128, :],
                                         comb_sb[:, qp * D:(qp + 1) * D]))
                continue
            P.wait("tensor", "ac", rexp_done)
            for par_h in range(2):
                P.row(par_h * 64)
                b0 = 6 if par_h == 0 else 0
                for s in range(2):
                    for oi, h in enumerate((par_h, 2 + par_h, 4 + par_h,
                                            6 + par_h)):
                        P.run("tensor", lambda e, s=s, h=h, qp=qp, oi=oi,
                              b0=b0: e.matmul(
                            bank(b0, 1024)[:, s * 512:(s + 1) * 512],
                            prt_sb[(h % 2) * 64:(h % 2) * 64 + 64,
                                   qp * 512 + (h // 2) * 128:
                                   qp * 512 + (h // 2) * 128 + 128],
                            gvp_sb[(h % 2) * 64:(h % 2) * 64 + 64,
                                   (h // 2) * D + s * 512:
                                   (h // 2) * D + (s + 1) * 512],
                            start=(oi == 0), stop=(oi == 3),
                            tile_position=((h % 2) * 64, 0)),
                            s="pe" if (oi == 3 and s == 1 and par_h == 1)
                              else None)
            rpv_done = P.val("pe")

            if ABL == 5:
                P.wait("vector", "pe", rpv_done)
                if qi >= 2:
                    P.waitp("vector", comb_stored[qi - 2])
                comb_ev = P.run("vector", lambda e, qp=qp: e.memset(
                    comb_sb[:, qp * D:(qp + 1) * D], 0.0), s="dv")
                P.drain("vector")
                prev_comb_ev[0] = comb_ev
                P.wait("scalar", "dv", comb_ev)
                comb_stored[qi] = (f"cb{qp}",
                                   P.dma("scalar", f"cb{qp}",
                                         combsp[qi * 128:(qi + 1) * 128, :],
                                         comb_sb[:, qp * D:(qp + 1) * D]))
                continue
            P.wait("vector", "pe", rpv_done)
            P.wait("vector", "gtq0", gt_sq)
            if qi >= 2:
                P.waitp("vector", comb_stored[qi - 2])
            P.run("vector", lambda e, qp=qp, par=par, jj=jj: e.tensor_tensor(
                out=rem_sb[:, qp * D:(qp + 1) * D]
                    .rearrange("p (h d) -> p h d", d=64),
                in0=bank(0, 1024).rearrange("p (h d) -> p h d", d=64),
                in1=gates_tm[:, par * 128 + jj * 32 + 16:
                             par * 128 + jj * 32 + 32]
                    .unsqueeze(2).broadcast_to([128, 16, 64]),
                op=ALU.mult), s="dv")
            prev_m1[0] = P.val("dv")
            for b in range(4):
                P.run("vector", lambda e, b=b: e.tensor_copy(
                    out=csc[:, 16 + b * 4:16 + (b + 1) * 4],
                    in_=bank(2 + b)[:, 0:260]
                        .rearrange("p (h c) -> p h c", c=65)[:, :, 64:65]
                        .squeeze(2)))
            P.drain("vector")
            P.run("vector", lambda e: e.reciprocal(out=csc[:, 0:8],
                                                   in_=csc[:, 16:24]))
            P.drain("vector")
            P.run("vector", lambda e: e.reciprocal(out=csc[:, 32:40],
                                                   in_=csc[:, 24:32]))
            P.drain("vector")
            P.run("vector", lambda e, par=par, jj=jj: e.tensor_tensor(
                out=c_l[:, 0:8], in0=csc[:, 0:8],
                in1=gates_tm[:, par * 128 + jj * 32:par * 128 + jj * 32 + 8],
                op=ALU.mult))
            P.run("vector", lambda e, par=par, jj=jj: e.tensor_tensor(
                out=c_l[:, 8:16], in0=csc[:, 32:40],
                in1=gates_tm[:, par * 128 + jj * 32 + 8:
                             par * 128 + jj * 32 + 16],
                op=ALU.mult))
            P.drain("vector")
            for b in range(4):
                P.run("vector", lambda e, b=b, qp=qp: e.tensor_tensor(
                    out=comb_sb[:, qp * D + b * 256:qp * D + (b + 1) * 256]
                        .rearrange("p (h d) -> p h d", d=64),
                    in0=bank(2 + b)[:, 0:260]
                        .rearrange("p (h c) -> p h c", c=65)[:, :, 0:64],
                    in1=c_l[:, b * 4:(b + 1) * 4].unsqueeze(2)
                        .broadcast_to([128, 4, 64]),
                    op=ALU.mult), s="dv")
            prev_m2[0] = P.val("dv")
            P.drain("vector")
            P.run("vector", lambda e, qp=qp: e.tensor_tensor(
                out=comb_sb[:, qp * D:(qp + 1) * D],
                in0=comb_sb[:, qp * D:(qp + 1) * D],
                in1=rem_sb[:, qp * D:(qp + 1) * D], op=ALU.add))
            P.drain("vector")
            P.run("vector", lambda e, qp=qp, par=par, jj=jj: e.tensor_tensor(
                out=rem_sb[:, qp * D:(qp + 1) * D]
                    .rearrange("p (h d) -> p h d", d=64),
                in0=bank(6, 1024).rearrange("p (h d) -> p h d", d=64),
                in1=gates_tm[:, par * 128 + jj * 32 + 16:
                             par * 128 + jj * 32 + 32]
                    .unsqueeze(2).broadcast_to([128, 16, 64]),
                op=ALU.mult), s="dv")
            prev_m3[0] = P.val("dv")
            P.drain("vector")
            comb_ev = P.run("vector", lambda e, qp=qp: e.tensor_tensor(
                out=comb_sb[:, qp * D:(qp + 1) * D],
                in0=comb_sb[:, qp * D:(qp + 1) * D],
                in1=rem_sb[:, qp * D:(qp + 1) * D], op=ALU.add), s="dv")
            P.drain("vector")
            prev_comb_ev[0] = comb_ev
            P.wait("scalar", "dv", comb_ev)
            comb_stored[qi] = (f"cb{qp}",
                               P.dma("scalar", f"cb{qp}",
                                     combsp[qi * 128:(qi + 1) * 128, :],
                                     comb_sb[:, qp * D:(qp + 1) * D]))
        att_dv_end[t] = P.val("dv")

    pe_endB = P.marker("tensor", "pe")
    ac_endB = P.marker("scalar", "ac")
    dv_endB = P.marker("vector", "dv")
    for eng in ("vector", "gpsimd", "sync", "scalar", "tensor"):
        P.wait(eng, "pe", pe_endB)
        P.wait(eng, "ac", ac_endB)
        P.wait(eng, "dv", dv_endB)
        P.waitp(eng, comb_stored[NC_TOK - 1])
        P.waitp(eng, comb_stored[NC_TOK - 2])
        P.wait(eng, "xqsp", xq_spilled[NC_TOK - 1])
        P.waitp(eng, xn_spilled[NC_TOK - 1])

    if ABL in (1, 2) and int(os.environ.get("K_ABL_PHC", "1")):
        out_stored = [0] * NC_TOK
        for ch in range(NC_TOK):
            row0 = ch * 128
            P.run("vector", lambda e: e.memset(out_sb[:, 0:D], 0.0))
            P.drain("vector")
            od = P.marker("vector", "dv")
            P.wait("scalar", "dv", od)
            out_stored[ch] = P.dma("scalar", "outst", y_h[row0:row0 + 128, :],
                                   out_sb[:, 0:D])
        P.wait("gpsimd", "outst", out_stored[NC_TOK - 1])
        with nc.Block() as block:
            @block.sync
            def _(e):
                for fn in P.ops["sync"]:
                    fn(e)

            @block.scalar
            def _(e):
                for fn in P.ops["scalar"]:
                    fn(e)

            @block.vector
            def _(e):
                for fn in P.ops["vector"]:
                    fn(e)

            @block.tensor
            def _(e):
                for fn in P.ops["tensor"]:
                    fn(e)

            @block.gpsimd
            def _(e):
                for fn in P.ops["gpsimd"]:
                    fn(e)
        return nc

    def wprep(src_h, nat_h, nrow, ncol):
        nch = nrow // 128
        nb = ncol // 16
        last = 0
        prev_store = P.val("wpst")
        for c in range(nch):
            ld = P.dma("gpsimd", "wpl", wqf[:, 0:ncol],
                       src_h[c * 128:(c + 1) * 128, :])
            P.wait("vector", "wpl", ld)
            P.run("vector", lambda e, nb=nb, ncol=ncol: e.tensor_reduce(
                out=wqbm[:, 0:nb],
                in_=wqf[:, 0:ncol].rearrange("p (b k) -> p b k", k=16),
                axis=AX.X, op=ALU.max, apply_absolute_value=True))
            P.drain("vector")
            P.run("vector", lambda e, nb=nb: e.tensor_scalar(
                out=wqs[:, 0:nb], in0=wqbm[:, 0:nb],
                scalar1=float(np.float32(1.0 / 6.0)), scalar2=1e-12,
                op0=ALU.mult, op1=ALU.max))
            P.drain("vector")
            P.run("vector", lambda e, nb=nb: e.reciprocal(
                out=wqinv[:, 0:nb], in_=wqs[:, 0:nb]))
            P.drain("vector")
            P.run("vector", lambda e, nb=nb: e.tensor_scalar(
                out=wqhalf[:, 0:nb], in0=wqs[:, 0:nb], scalar1=0.5,
                scalar2=None, op0=ALU.mult))
            P.drain("vector")
            P.run("vector", lambda e, nb=nb, ncol=ncol: e.tensor_tensor(
                out=wqy[:, 0:ncol].rearrange("p (b k) -> p b k", k=16),
                in0=wqf[:, 0:ncol].rearrange("p (b k) -> p b k", k=16),
                in1=wqinv[:, 0:nb].unsqueeze(2).broadcast_to([128, nb, 16]),
                op=ALU.mult))
            P.drain("vector")
            P.run("vector", lambda e, ncol=ncol: e._custom_dve(
                FP4A, out=wqa[:, 0:ncol], in0=wqy[:, 0:ncol], in1=c3c[:],
                s0=0.0625, s1=0.5625, imm2=1.5625))
            P.drain("vector")
            P.run("vector", lambda e, ncol=ncol: e._custom_dve(
                FP4B, out=wqf[:, 0:ncol], in0=wqy[:, 0:ncol],
                in1=wqa[:, 0:ncol], s0=6.25, s1=12.25))
            P.drain("vector")
            P.run("vector", lambda e, ncol=ncol: e._custom_dve(
                FP4C, out=wqa[:, 0:ncol], in0=wqf[:, 0:ncol],
                in1=wqy[:, 0:ncol], s0=25.0))
            P.drain("vector")
            P.wait("vector", "wpst", prev_store)
            wq_done = P.run("vector", lambda e, nb=nb, ncol=ncol:
                e.tensor_tensor(
                    out=wqout[:, 0:ncol].rearrange("p (b k) -> p b k", k=16),
                    in0=wqa[:, 0:ncol].rearrange("p (b k) -> p b k", k=16),
                    in1=wqhalf[:, 0:nb].unsqueeze(2)
                        .broadcast_to([128, nb, 16]),
                    op=ALU.mult), s="dv")
            P.drain("vector")
            P.wait("gpsimd", "dv", wq_done)
            prev_store = P.dma("gpsimd", "wpst",
                               nat_h[c * 128:(c + 1) * 128, :],
                               wqout[:, 0:ncol])
            last = prev_store
        return last

    wg_nat_done = wprep(wg_h, wgnat, I, D)
    wu_nat_done = wprep(wu_h, wunat, I, D)
    wd_nat_done = wprep(wd_h, wdnat, D, I)
    wprep_all = max(wg_nat_done, wu_nat_done, wd_nat_done)

    wfm_sq = 0
    for (nat, dst, nch, ncol) in ((wgnat, wgq_fm, 8, I), (wunat, wuq_fm, 8, I),
                                  (wdnat, wdq_fm, 22, D)):
        for c in range(nch):
            P.wait("sync", "wpst", wprep_all)
            for r0 in range(0, ncol, 512):
                rw = min(512, ncol - r0)
                wfm_sq = P.dma("sync", "wfmq0",
                               dst[:, c * ncol + r0:c * ncol + r0 + rw],
                               nat[r0:r0 + rw, c * 128:(c + 1) * 128],
                               transpose=True)
    wout_loaded = 0
    for c in range(8):
        wout_loaded = P.dma("gpsimd", "wo2", wout_s2[:, c * D:(c + 1) * D],
                            wout_h[c * 128:(c + 1) * 128, :])

    hq_spilled = [0] * NC_TOK
    out_stored = [0] * NC_TOK
    gu_pe_end = [0] * NT
    dn_pe_end = [0] * NT

    P.wait("tensor", "wfmq0", wfm_sq)
    P.wait("tensor", "wo2", wout_loaded)

    def emit_quant(ncol, nb, xin, xout, yb, ab, bb):
        P.run("vector", lambda e: e.tensor_reduce(
            out=qbm[:, 0:nb], in_=xin.rearrange("p (b k) -> p b k", k=16),
            axis=AX.X, op=ALU.max, apply_absolute_value=True))
        P.drain("vector")
        P.run("vector", lambda e: e.tensor_scalar(
            out=qs[:, 0:nb], in0=qbm[:, 0:nb],
            scalar1=float(np.float32(1.0 / 6.0)), scalar2=1e-12,
            op0=ALU.mult, op1=ALU.max))
        P.drain("vector")
        P.run("vector", lambda e: e.reciprocal(
            out=qbm[:, 0:nb], in_=qs[:, 0:nb]))
        P.drain("vector")
        P.run("vector", lambda e: e.tensor_copy(
            out=qinvb[:, 0:nb], in_=qbm[:, 0:nb]))
        P.run("vector", lambda e: e.tensor_scalar(
            out=qhalf[:, 0:nb], in0=qs[:, 0:nb], scalar1=0.5, scalar2=None,
            op0=ALU.mult))
        P.drain("vector")
        P.run("vector", lambda e: e.tensor_tensor(
            out=yb.rearrange("p (b k) -> p b k", k=16),
            in0=xin.rearrange("p (b k) -> p b k", k=16),
            in1=qinvb[:, 0:nb].unsqueeze(2).broadcast_to([128, nb, 16]),
            op=ALU.mult))
        P.drain("vector")
        P.run("vector", lambda e: e._custom_dve(
            FP4A, out=ab, in0=yb, in1=c3c[:],
            s0=0.0625, s1=0.5625, imm2=1.5625))
        P.drain("vector")
        P.run("vector", lambda e: e._custom_dve(
            FP4B, out=bb, in0=yb, in1=ab, s0=6.25, s1=12.25))
        P.drain("vector")
        P.run("vector", lambda e: e._custom_dve(
            FP4C, out=ab, in0=bb, in1=yb, s0=25.0))
        P.drain("vector")
        tgt = P.run("vector", lambda e: e.tensor_tensor(
            out=xout.rearrange("p (b k) -> p b k", k=16),
            in0=ab.rearrange("p (b k) -> p b k", k=16),
            in1=qhalf[:, 0:nb].unsqueeze(2).broadcast_to([128, nb, 16]),
            op=ALU.mult), s="dv")
        P.drain("vector")
        return tgt

    for t in range(NT):
        if t >= 1:
            P.wait("sync", "pe", dn_pe_end[t - 1])
        xq_sq = cb_sq = 0
        for c in range(8):
            P.wait("sync", "xqsp", xq_spilled[t * 4 + 3])
            xq_sq = P.dma("sync", "xqfq0", xqfm2[:, c * TILE:(c + 1) * TILE],
                          xqsp[t * TILE:(t + 1) * TILE,
                               c * 128:(c + 1) * 128],
                          transpose=True)
            cb_sq = P.dma("sync", "cbfq0", combfm[:, c * TILE:(c + 1) * TILE],
                          combsp[t * TILE:(t + 1) * TILE,
                                 c * 128:(c + 1) * 128],
                          transpose=True)

        P.row("F")
        P.wait("tensor", "xqfq0", xq_sq)
        if _spk == "ExternalOutput" and t == 0:
            P.wait("gpsimd", "xqfq0", xq_sq)
            P.wait("gpsimd", "wfmq0", wfm_sq)
            P.dma("gpsimd", "wo2", wgfmd[:], wgq_fm[:])
            P.dma("gpsimd", "wo2", xqfmd[:], xqfm2[:])
        prev_h = 0
        for j in range(4):
            ch = t * 4 + j
            row0 = ch * 128
            for s in range(6):
                w = 512 if s < 5 else 256
                col = s * 512
                P.wait("tensor", "dv", prev_h)
                for k in range(8):
                    P.run("tensor", lambda e, j=j, k=k, col=col, w=w: e.matmul(
                        bank(0, w),
                        xqfm2[:, k * TILE + j * 128:k * TILE + (j + 1) * 128],
                        wgq_fm[:, k * I + col:k * I + col + w],
                        start=(k == 0), stop=(k == 7)))
                    P.run("tensor", lambda e, j=j, k=k, col=col, w=w: e.matmul(
                        bank(1, w),
                        xqfm2[:, k * TILE + j * 128:k * TILE + (j + 1) * 128],
                        wuq_fm[:, k * I + col:k * I + col + w],
                        start=(k == 0), stop=(k == 7)),
                        s="pe" if k == 7 else None)
                pe_gu = P.val("pe")
                P.wait("scalar", "pe", pe_gu)
                P.wait("scalar", "dv", prev_h)
                sg_done = P.run("scalar", lambda e, w=w: e.activation(
                    out=sg_sb[:, 0:w], in_=bank(0, w),
                    func=AF.Silu, bias=0.0, scale=1.0), s="ac")
                P.wait("vector", "ac", sg_done)
                prev_h = P.run("vector", lambda e, col=col, w=w: e.tensor_tensor(
                    out=h_sb[:, col:col + w], in0=bank(1, w),
                    in1=sg_sb[:, 0:w], op=ALU.mult), s="dv")
                P.drain("vector")
            gu_pe_end[t] = P.val("pe")
            if ch >= 1:
                P.wait("gpsimd", "hqsp", hq_spilled[ch - 1])
                P.wait("vector", "hqsp", hq_spilled[ch - 1])
            if int(os.environ.get("K_SKIP_HQ", "0")):
                P.wait("vector", "dv", prev_h)
                hq_done = P.run("vector", lambda e: e.tensor_copy(
                    out=qy[:, 0:I], in_=h_sb[:, 0:I]), s="dv")
                P.drain("vector")
            else:
                hq_done = emit_quant(I, NBI, h_sb[:, 0:I], qy[:, 0:I],
                                     qy[:, 0:I], qa[:, 0:I], h_sb[:, 0:I])
            P.wait("gpsimd", "dv", hq_done)
            hq_spilled[ch] = P.dma("gpsimd", "hqsp", hqsp[row0:row0 + 128, :],
                                   qy[:, 0:I])

        P.wait("tensor", "cbfq0", cb_sq)
        prev_out_done = 0
        hq_guard = dn_pe_end[t - 1] if t >= 1 else 0
        for half in range(2):
            f0 = half * 11
            hq_sq = 0
            for c in range(11):
                fc = f0 + c
                P.wait("sync", "hqsp", hq_spilled[t * 4 + 3])
                P.wait("sync", "pe", hq_guard)
                hq_sq = P.dma("sync", "hqfq0", hqfm[:, c * TILE:(c + 1) * TILE],
                              hqsp[t * TILE:(t + 1) * TILE,
                                   fc * 128:(fc + 1) * 128],
                              transpose=True)
            P.wait("tensor", "hqfq0", hq_sq)
            for j in range(4):
                ch = t * 4 + j
                row0 = ch * 128
                if half == 0:
                    P.wait("tensor", "dv", P.val("dv") if j == 0 else 0)
                    for s in range(2):
                        for k in range(8):
                            P.run("tensor", lambda e, j=j, s=s, k=k: e.matmul(
                                bank(2 * j + s),
                                combfm[:, k * TILE + j * 128:
                                       k * TILE + (j + 1) * 128],
                                wout_s2[:, k * D + s * 512:
                                        k * D + (s + 1) * 512],
                                start=(k == 0), stop=False))
                        for c in range(11):
                            P.run("tensor", lambda e, j=j, s=s, c=c,
                                  f0=f0: e.matmul(
                                bank(2 * j + s),
                                hqfm[:, c * TILE + j * 128:
                                     c * TILE + (j + 1) * 128],
                                wdq_fm[:, (f0 + c) * D + s * 512:
                                       (f0 + c) * D + (s + 1) * 512],
                                start=False, stop=False),
                                s="pe" if c == 10 else None)
                else:
                    for s in range(2):
                        for c in range(11):
                            P.run("tensor", lambda e, j=j, s=s, c=c,
                                  f0=f0: e.matmul(
                                bank(2 * j + s),
                                hqfm[:, c * TILE + j * 128:
                                     c * TILE + (j + 1) * 128],
                                wdq_fm[:, (f0 + c) * D + s * 512:
                                       (f0 + c) * D + (s + 1) * 512],
                                start=False, stop=(c == 10)),
                                s="pe" if c == 10 else None)
                    pe_out = P.val("pe")
                    P.wait("gpsimd", "dv", prev_out_done)
                    ldx2 = P.dma("gpsimd", "x2l", x2_sb[:, 0:D],
                                 x_h[row0:row0 + 128, :])
                    P.wait("vector", "pe", pe_out)
                    P.wait("vector", "x2l", ldx2)
                    if ch >= 1:
                        P.wait("vector", "outst", out_stored[ch - 1])
                    out_done = P.run("vector",
                        lambda e, j=j: e.tensor_tensor(
                            out=out_sb[:, 0:D],
                            in0=bank(2 * j, 1024),
                            in1=x2_sb[:, 0:D], op=ALU.add), s="dv")
                    P.drain("vector")
                    prev_out_done = out_done
                    P.wait("scalar", "dv", out_done)
                    out_stored[ch] = P.dma("scalar", "outst",
                                           y_h[row0:row0 + 128, :],
                                           out_sb[:, 0:D])
            hq_guard = P.val("pe")
            dn_pe_end[t] = P.val("pe")
        P.wait("tensor", "dv", prev_out_done)

    P.wait("gpsimd", "outst", out_stored[NC_TOK - 1])
    P.wait("sync", "outst", out_stored[NC_TOK - 1])

    with nc.Block() as block:
        @block.sync
        def _(e):
            for fn in P.ops["sync"]:
                fn(e)

        @block.scalar
        def _(e):
            for fn in P.ops["scalar"]:
                fn(e)

        @block.vector
        def _(e):
            for fn in P.ops["vector"]:
                fn(e)

        @block.tensor
        def _(e):
            for fn in P.ops["tensor"]:
                fn(e)

        @block.gpsimd
        def _(e):
            for fn in P.ops["gpsimd"]:
                fn(e)

    return nc


def kernel(L0_attn_x, L0_attn_wq, L0_attn_wk, L0_attn_wv, L0_attn_wgq,
           L0_attn_wgk, L0_attn_wgv, L0_attn_r_out_w, L0_attn_gate_local_w,
           L0_attn_gate_remote_w, L0_attn_w_out, L0_ffn_w_gate, L0_ffn_w_up,
           L0_ffn_w_down):
    B, S, Dm = L0_attn_x.shape
    ncore = 8
    per = B // ncore
    nc_prog = build_core(nseq=per, seqlen=S)
    mybir.codegen_inst_isa_subclasses(nc_prog)
    shards = [np.ascontiguousarray(
        L0_attn_x[c * per:(c + 1) * per].reshape(per * S, Dm), dtype=np.float32)
        for c in range(ncore)]
    w = dict(
        wq=np.ascontiguousarray(L0_attn_wq, np.float32),
        wk=np.ascontiguousarray(L0_attn_wk, np.float32),
        wv=np.ascontiguousarray(L0_attn_wv, np.float32),
        wgq=np.ascontiguousarray(L0_attn_wgq, np.float32),
        wgk=np.ascontiguousarray(L0_attn_wgk, np.float32),
        wgv=np.ascontiguousarray(L0_attn_wgv, np.float32),
        rout=np.ascontiguousarray(L0_attn_r_out_w, np.float32),
        glw=np.ascontiguousarray(L0_attn_gate_local_w, np.float32),
        grw=np.ascontiguousarray(L0_attn_gate_remote_w, np.float32),
        wout=np.ascontiguousarray(L0_attn_w_out, np.float32),
        wgate=np.ascontiguousarray(L0_ffn_w_gate, np.float32),
        wup=np.ascontiguousarray(L0_ffn_w_up, np.float32),
        wdown=np.ascontiguousarray(L0_ffn_w_down, np.float32),
    )
    in_maps = [dict(w, x=shards[c]) for c in range(ncore)]
    res = run_bass_kernel_spmd(nc_prog, in_maps, core_ids=list(range(ncore)))
    out = np.stack([np.asarray(res.results[c]["y"]).reshape(per, S, Dm)
                    for c in range(ncore)], 0)
    return out.reshape(B, S, Dm).astype(np.float32)


# version 08 (final)
# speedup vs baseline: 1.1339x; 1.1339x over previous
import numpy as np
import ml_dtypes

import concourse.bass as bass
import concourse.mybir as mybir
from concourse.bass_isa import InstPseudoReloadLibraryIndex, ReduceOp
from concourse.bass_utils import run_bass_kernel_spmd

F32 = mybir.dt.float32
BF16 = mybir.dt.bfloat16
ALU = mybir.AluOpType
AF = mybir.ActivationFunctionType
AX = mybir.AxisListType

from concourse.dve_op import DveOp
from concourse.dve_spec import (Spec, Src0, Src1, C0, C1, C2, C3, Zero, lower,
                                select, sq)
import concourse.dve_ops as dve_ops


def _cdt(a):
    return np.float64 if a.dtype == np.float64 else np.float32


def _mk_dveop(name, body, reference, perf_en=True):
    spec = Spec(body=body)
    shas, pe = {}, {}
    for ver in ("v3", "v4"):
        try:
            c = lower(spec, ver=ver, perf_en=perf_en, name=name)
            shas[ver], pe[ver] = c.sha(ver), perf_en
        except Exception:
            c = lower(spec, ver=ver, perf_en=False, name=name)
            shas[ver], pe[ver] = c.sha(ver), False
    return DveOp(name, spec, subdim=False, uops_sha=shas, perf_en=pe,
                 reference=reference)


def _ref_fp4a(in0, in1, s0, s1, imm2):
    f = _cdt(in0)
    y2 = (in0.astype(f) * in0.astype(f)).astype(f)
    c3 = np.asarray(in1, dtype=f)
    return (((y2 > s0).astype(f) + (y2 > s1).astype(f))
            + ((y2 > imm2).astype(f) + (y2 > c3).astype(f))).astype(f)


def _ref_fp4b(in0, in1, s0, s1, imm2):
    f = _cdt(in0)
    y2 = (in0.astype(f) * in0.astype(f)).astype(f)
    t = (y2 > s0).astype(f) + (y2 > s1).astype(f)
    return ((t + t) + in1.astype(f)).astype(f)


def _ref_fp4c(in0, in1, s0, s1, imm2):
    f = _cdt(in0)
    y = in1.astype(f)
    y2 = (y * y).astype(f)
    d = (y2 > s0).astype(f)
    e = d + d
    g = in0.astype(f) + (e + e)
    return np.where(y > 0, g, -g).astype(f)


_y2 = sq(Src0)
FP4A = _mk_dveop("FP4LOW",
                 ((_y2 > C0) + (_y2 > C1)) + ((_y2 > C2) + (_y2 > C3)),
                 _ref_fp4a)
_t = (_y2 > C0) + (_y2 > C1)
FP4B = _mk_dveop("FP4MID", (_t + _t) + Src1, _ref_fp4b)
_y2b = sq(Src1)
_d = (_y2b > C0)
_e = _d + _d
_g = Src0 + (_e + _e)
FP4C = _mk_dveop("FP4SGN", select(Src1 > Zero, _g, Zero - _g), _ref_fp4c)

_REGISTERED = False


def _register_fp4():
    global _REGISTERED
    if _REGISTERED:
        return
    for op in (FP4A, FP4B, FP4C):
        if not any(o.name == op.name for o in dve_ops.OPS):
            dve_ops.OPS.append(op)
    if dve_ops._CUSTOM_DVE_ROW_BASE + len(dve_ops.OPS) - 1 >= 0x20:
        dve_ops.OPS[:] = [o for o in dve_ops.OPS
                          if not o.name.startswith("DVEHASH32L")]
    dve_ops._SUB_OPCODE_FOR_NAME.clear()
    dve_ops._SUB_OPCODE_FOR_NAME.update({
        op.name: dve_ops._CUSTOM_DVE_ROW_BASE + i
        for i, op in enumerate(dve_ops.OPS)})
    for op in (FP4A, FP4B, FP4C):
        dve_ops.CUSTOM_DVE_REFS[op.name] = op.ref
        try:
            dve_ops.CUSTOM_DVE_SPECS[op.name] = op.spec
        except Exception:
            pass
    _REGISTERED = True


ENGS = ("sync", "scalar", "vector", "tensor", "gpsimd")


class Plan:
    def __init__(self, nc):
        self.nc = nc
        self.ops = {e: [] for e in ENGS}
        self.sems = {}
        self.hiw = {}

    def sem(self, name):
        h = self.nc.semaphore(name).__enter__()
        self.sems[name] = [h, 0]

    def inc(self, s, k):
        self.sems[s][1] += k
        return self.sems[s][1]

    def val(self, s):
        return self.sems[s][1]

    def add(self, eng, fn):
        self.ops[eng].append(fn)

    def wait(self, eng, s, tgt):
        if tgt is None or tgt <= 0:
            return
        key = (eng, s)
        if self.hiw.get(key, -1) >= tgt:
            return
        self.hiw[key] = tgt
        h = self.sems[s][0]
        self.add(eng, lambda e: e.wait_ge(h, tgt))

    def dma(self, eng, s, out, in_, transpose=False):
        h = self.sems[s][0]
        tgt = self.inc(s, 16)
        if transpose:
            self.add(eng,
                     lambda e: e.dma_start_transpose(out, in_).then_inc(h, 16))
        else:
            self.add(eng,
                     lambda e: e.dma_start(out=out, in_=in_).then_inc(h, 16))
        return tgt

    def run(self, eng, fn, s=None, k=1):
        if s is None:
            self.add(eng, fn)
            return None
        h = self.sems[s][0]
        tgt = self.inc(s, k)
        self.add(eng, lambda e: fn(e).then_inc(h, k))
        return tgt

    def drain(self, eng):
        self.add(eng, lambda e: e.drain())

    def waitp(self, eng, pair):
        if pair:
            self.wait(eng, pair[0], pair[1])

    def row(self, r):
        if getattr(self, "_pe_row", "F") != r:
            for _ in range(8):
                self.add("tensor", lambda e: e.nop())
            self._pe_row = r

    def marker(self, eng, s):
        h = self.sems[s][0]
        tgt = self.inc(s, 1)
        self.add(eng, lambda e: e.drain())
        self.add(eng, lambda e: e.sem_inc(h, 1))
        return tgt


D = 1024
H, HD = 16, 64
LH = 8
GD = 512
I = 2816
NBI = I // 16
TILE = 512


def build_core(nseq=2, seqlen=4096):
    import os
    ABL = int(os.environ.get("K_ABL", "0"))
    _register_fp4()
    T = nseq * seqlen
    NT = T // TILE
    TPS = seqlen // TILE
    NC_TOK = T // 128
    LATB = 64 * (seqlen // 4096) if seqlen >= 4096 else max(seqlen // 64, 8)
    LATB = seqlen // 64

    nc = bass.Bass("TRN2", target_bir_lowering=False)

    x_h = nc.dram_tensor("x", [T, D], F32, kind="ExternalInput").ap()
    wq_h = nc.dram_tensor("wq", [D, D], F32, kind="ExternalInput").ap()
    wk_h = nc.dram_tensor("wk", [D, D], F32, kind="ExternalInput").ap()
    wv_h = nc.dram_tensor("wv", [D, D], F32, kind="ExternalInput").ap()
    wgq_h = nc.dram_tensor("wgq", [D, GD], F32, kind="ExternalInput").ap()
    wgk_h = nc.dram_tensor("wgk", [D, GD], F32, kind="ExternalInput").ap()
    wgv_h = nc.dram_tensor("wgv", [D, GD], F32, kind="ExternalInput").ap()
    rout_h = nc.dram_tensor("rout", [GD, D], F32, kind="ExternalInput").ap()
    glw_h = nc.dram_tensor("glw", [H, HD], F32, kind="ExternalInput").ap()
    grw_h = nc.dram_tensor("grw", [H, HD], F32, kind="ExternalInput").ap()
    wout_h = nc.dram_tensor("wout", [D, D], F32, kind="ExternalInput").ap()
    wg_h = nc.dram_tensor("wgate", [I, D], F32, kind="ExternalInput").ap()
    wu_h = nc.dram_tensor("wup", [I, D], F32, kind="ExternalInput").ap()
    wd_h = nc.dram_tensor("wdown", [D, I], F32, kind="ExternalInput").ap()
    y_h = nc.dram_tensor("y", [T, D], F32, kind="ExternalOutput").ap()
    _spk = "ExternalOutput" if int(os.environ.get("K_DBG", "0")) else "Internal"
    xnsp = nc.dram_tensor("xnsp", [T, D], BF16, kind=_spk).ap()
    combsp = nc.dram_tensor("combsp", [T, D], BF16, kind=_spk).ap()
    xqsp = nc.dram_tensor("xqsp", [T, D], BF16, kind=_spk).ap()
    hqsp = nc.dram_tensor("hqsp", [T, I], BF16, kind=_spk).ap()
    wgnat = nc.dram_tensor("wgnat", [I, D], BF16, kind=_spk).ap()
    if _spk == "ExternalOutput":
        wgfmd = nc.dram_tensor("wgfmd", [128, 8 * I], BF16, kind=_spk).ap()
        xqfmd = nc.dram_tensor("xqfmd", [128, 8 * TILE], BF16, kind=_spk).ap()
    wunat = nc.dram_tensor("wunat", [I, D], BF16, kind=_spk).ap()
    wdnat = nc.dram_tensor("wdnat", [D, I], BF16, kind=_spk).ap()

    BASE = 16384 + 64
    TOP = 224 * 1024 - 64

    class Lay:
        def __init__(self, start):
            self.cur = start

        def take(self, name, shape, dtype, nbytes_pp):
            off = self.cur
            self.cur = (self.cur + nbytes_pp + 31) & ~31
            return nc.alloc_sbuf_tensor_at(name, shape, dtype, offset=off)

        def t(self, name, p, n, dtype):
            size = n * (4 if dtype == F32 else 2)
            return self.take(name, [p, n], dtype, size)

    st = Lay(BASE)
    iota_p = st.t("iota_p", 128, 1, F32)
    iota_p1 = st.t("iota_p1", 128, 1, F32)
    ends_r = st.t("ends_r", 128, TPS * 4, BF16)
    ident = st.t("ident", 128, 128, BF16)
    cend = st.t("cend", 128, 2, F32)
    c3c = st.t("c3c", 128, 1, F32)
    epsc = st.t("epsc", 128, 1, F32)
    csc = st.t("csc", 128, 40, F32)
    cacc = st.t("cacc", 128, 20, F32)
    UBASE = st.cur

    pb = Lay(UBASE)
    wq_s = pb.t("wq_s", 128, 8 * D, BF16)
    wk_s = pb.t("wk_s", 128, 8 * D, BF16)
    wv_s = pb.t("wv_s", 128, 8 * D, BF16)
    wgq_s = pb.t("wgq_s", 128, 8 * GD, BF16)
    wgk_s = pb.t("wgk_s", 128, 8 * GD, BF16)
    wgv_s = pb.t("wgv_s", 128, 8 * GD, BF16)
    rout_s = pb.t("rout_s", 128, 4 * D, BF16)
    gdiag_s = pb.t("gdiag_s", 128, 8 * 32, BF16)
    xnfm = pb.t("xnfm", 128, 8 * TILE, BF16)
    q_sb = pb.t("q_sb", 128, 8 * TILE, BF16)
    gq_sb = pb.t("gq_sb", 128, 4 * TILE, BF16)
    kring = pb.t("kring", 128, 8 * 1024, BF16)
    NVS = 6
    vring = pb.t("vring", 128, NVS * H * 65, BF16)
    p_sb = pb.t("p_sb", 128, 8 * LATB, BF16)
    p32 = pb.t("p32", 128, 8 * 8, F32)
    gk_sb = pb.t("gk_sb", 128, 4 * LATB, BF16)
    gv_sb = pb.t("gv_sb", 128, 4 * LATB, BF16)
    gvp_sb = pb.t("gvp_sb", 128, 4 * D, BF16)
    gates_pre = pb.t("gates_pre", 32, 512, BF16)
    gates_tm = pb.t("gates_tm", 128, 2 * 4 * 32, BF16)
    msk_sb = pb.t("msk_sb", 128, H * 384, BF16)
    pt_sb = pb.t("pt_sb", 128, 2 * H * 384, BF16)
    prem = pb.t("prem", 128, 2 * LH * 64, BF16)
    prt_sb = pb.t("prt_sb", 128, 2 * 512, BF16)
    rsum_r = pb.t("rsum_r", 128, 8, F32)
    c_l = pb.t("c_l", 128, 16, BF16)
    rem_sb = pb.t("rem_sb", 128, 2 * D, BF16)
    comb_sb = pb.t("comb_sb", 128, 2 * D, BF16)
    NXB = 2
    x_sb = pb.t("x_sb", 128, NXB * D, F32)
    bns = pb.t("bns", 128, NXB * 12, F32)
    mv = pb.t("mv", 128, NXB * 2, F32)
    lnst = pb.t("lnst", 128, NXB * 4, F32)
    xn_sb = pb.t("xn_sb", 128, D, BF16)
    xnf32 = pb.t("xnf32", 128, D, F32)
    qaB = pb.t("qaB", 128, D, BF16)
    xqB = pb.t("xqB", 128, D, BF16)
    qbmB = pb.t("qbmB", 128, 64, F32)
    qsB = pb.t("qsB", 128, 64, F32)
    qinvB = qbmB
    qhalfB = qsB
    assert pb.cur <= TOP, f"phase-B over: {pb.cur}"

    pc = Lay(UBASE)
    wgq_fm = pc.t("wgq_fm", 128, 8 * I, BF16)
    wuq_fm = pc.t("wuq_fm", 128, 8 * I, BF16)
    wdq_fm = pc.t("wdq_fm", 128, 22 * D, BF16)
    wout_s2 = pc.t("wout_s2", 128, 8 * D, BF16)
    hqfm = pc.t("hqfm", 128, 11 * TILE, BF16)
    xqfm2 = pc.t("xqfm2", 128, 8 * TILE, BF16)
    combfm = pc.t("combfm", 128, 8 * TILE, BF16)
    h_sb = pc.t("h_sb", 128, I, BF16)
    qy = pc.t("qy", 128, I, BF16)
    qa = pc.t("qa", 128, I, BF16)
    xn2_sb = pc.t("xn2_sb", 128, D, BF16)
    qbm = pc.t("qbm", 128, NBI, F32)
    qs = pc.t("qs", 128, NBI, F32)
    qinvb = pc.t("qinvb", 128, NBI, BF16)
    qhalf = pc.t("qhalf", 128, NBI, BF16)
    x2_sb = pc.t("x2_sb", 128, D, F32)
    out_sb = pc.t("out_sb", 128, D, F32)
    sg_sb = pc.t("sg_sb", 128, 512, BF16)
    assert pc.cur <= TOP, f"phase-C over: {pc.cur}"

    wp = Lay(UBASE)
    wqf = wp.t("wqf", 128, I, F32)
    wqy = wp.t("wqy", 128, I, F32)
    wqa = wp.t("wqa", 128, I, F32)
    wqout = wp.t("wqout", 128, I, BF16)
    wqbm = wp.t("wqbm", 128, NBI, F32)
    wqs = wp.t("wqs", 128, NBI, F32)
    wqinv = wp.t("wqinv", 128, NBI, F32)
    wqhalf = wp.t("wqhalf", 128, NBI, F32)
    assert wp.cur <= UBASE + 8 * I * 2, "wprep scratch exceeds wgq_fm span"

    arena = nc.alloc_psum_tensor("arena", [128, 4096], F32)

    def bank(b, w=512):
        return arena[:, b * 512:b * 512 + w]

    P = Plan(nc)
    for s in ("pe", "dv", "ac", "gp",
              "xld0", "xld1", "xnsp0", "xnsp1", "xfq0", "xfq1",
              "gtq0", "gtq1", "cb0", "cb1", "outst",
              "wfmq0", "wfmq1", "xqfq0", "xqfq1", "cbfq0", "cbfq1",
              "hqfq0", "hqfq1",
              "wat", "wpl", "wpst", "wo2", "xn2l", "xqsp", "hqsp", "x2l"):
        P.sem(s)

    P.run("gpsimd", lambda e: e.iota(iota_p[:], pattern=[[0, 1]], base=0,
                                     channel_multiplier=1,
                                     allow_small_or_imprecise_dtypes=True))
    P.run("gpsimd", lambda e: e.iota(iota_p1[:], pattern=[[0, 1]], base=128,
                                     channel_multiplier=1,
                                     allow_small_or_imprecise_dtypes=True))
    for t in range(TPS):
        for j in range(4):
            base = (t * 512 + j * 128) // 64 + 1
            P.run("gpsimd", lambda e, c=t * 4 + j, b=base: e.iota(
                ends_r[0:64, c:c + 1], pattern=[[0, 1]], base=b,
                channel_multiplier=0, allow_small_or_imprecise_dtypes=True))
            P.run("gpsimd", lambda e, c=t * 4 + j, b=base + 1: e.iota(
                ends_r[64:128, c:c + 1], pattern=[[0, 1]], base=b,
                channel_multiplier=0, allow_small_or_imprecise_dtypes=True))
    P.run("gpsimd", lambda e: e.iota(ident[:], pattern=[[1, 128]], base=0,
                                     channel_multiplier=-1,
                                     allow_small_or_imprecise_dtypes=True))
    P.add("gpsimd", lambda e: e.add_instruction(InstPseudoReloadLibraryIndex(
        name=nc.get_next_instruction_name(), ins=[], outs=[], lib_index=1)))
    gp_setup = P.marker("gpsimd", "gp")
    P.wait("vector", "gp", gp_setup)
    P.run("vector", lambda e: e.tensor_scalar(out=ident[:], in0=ident[:],
                                              scalar1=0.0, scalar2=None,
                                              op0=ALU.is_equal))
    P.run("vector", lambda e: e.memset(cend[:, 0:1], 128.0))
    P.run("vector", lambda e: e.memset(cend[:, 1:2], 256.0))
    P.run("vector", lambda e: e.memset(c3c[:], 3.0625))
    P.run("vector", lambda e: e.memset(epsc[:], 1e-5))
    P.run("vector", lambda e: e.memset(
        vring[:].rearrange("p (s c) -> p s c", c=65)[:, :, 64:65], 1.0))
    P.run("vector", lambda e: e.memset(p_sb[:], 0.0))
    P.run("vector", lambda e: e.memset(gk_sb[:], 0.0))
    P.run("vector", lambda e: e.memset(gv_sb[:], 0.0))
    P.run("vector", lambda e: e.memset(gdiag_s[:], 0.0))
    P.run("vector", lambda e: e.memset(prem[:], 0.0))
    P.run("vector", lambda e: e.memset(gvp_sb[:], 0.0))
    dv_setup = P.marker("vector", "dv")

    def wload(dst, src, n_out, nch):
        for c in range(nch):
            P.dma("gpsimd", "wat", dst[:, c * n_out:(c + 1) * n_out],
                  src[c * 128:(c + 1) * 128, :])

    wload(wq_s, wq_h, D, 8)
    wload(wk_s, wk_h, D, 8)
    wload(wv_s, wv_h, D, 8)
    wload(wgq_s, wgq_h, GD, 8)
    wload(wgk_s, wgk_h, GD, 8)
    wload(wgv_s, wgv_h, GD, 8)
    wload(rout_s, rout_h, D, 4)
    P.wait("gpsimd", "dv", dv_setup)
    for h in range(H):
        c = h // 2
        r0 = (h % 2) * 64
        P.dma("gpsimd", "wat", gdiag_s[r0:r0 + 64, c * 32 + h:c * 32 + h + 1],
              glw_h[h, :].unsqueeze(1))
        P.dma("gpsimd", "wat",
              gdiag_s[r0:r0 + 64, c * 32 + 16 + h:c * 32 + 17 + h],
              grw_h[h, :].unsqueeze(1))
    w_attn_loaded = ("wat", P.val("wat"))

    vslot = lambda ch: ch % NVS
    xn_spilled = [None] * NC_TOK
    xq_spilled = [0] * NC_TOK
    comb_stored = [None] * NC_TOK
    proj_pe_end = [0] * NT
    att_dv_end = [0] * NT
    prev_comb_ev = [0]
    prev_mask_dv = [0]
    mask_dv_bp = [0, 0]
    prev_m1 = [0]
    prev_m2 = [0]
    prev_m3 = [0]
    prev_exp = [0]
    prev_gt_sq = [0]
    prev_pv = [0, 0]
    prev_prtrans = [0, 0]
    prev_pool = [0]

    for t in range(NT):
        par = t % 2
        sq_i = t % TPS
        P.wait("tensor", "dv", prev_m3[0])
        if sq_i == 0 and t > 0:
            P.wait("vector", "pe", P.val("pe"))
            P.run("vector", lambda e: e.memset(p_sb[:], 0.0))
            P.run("vector", lambda e: e.memset(gk_sb[:], 0.0))
            P.run("vector", lambda e: e.memset(gv_sb[:], 0.0))
            P.drain("vector")

        for j in range(4):
            ch = t * 4 + j
            sl = ch % NXB
            row0 = ch * 128
            if ch >= NXB:
                P.wait("scalar", "xqsp", xq_spilled[ch - NXB])
            ld_x = P.dma("scalar", f"xld{sl}", x_sb[:, sl * D:(sl + 1) * D],
                         x_h[row0:row0 + 128, :])
            P.wait("vector", f"xld{sl}", ld_x)
            P.wait("vector", "dv", dv_setup)
            P.run("vector", lambda e, sl=sl: e.bn_stats(
                bns[:, sl * 12:sl * 12 + 6], x_sb[:, sl * D:sl * D + 512]))
            P.run("vector", lambda e, sl=sl: e.bn_stats(
                bns[:, sl * 12 + 6:sl * 12 + 12],
                x_sb[:, sl * D + 512:sl * D + 1024]))
            P.drain("vector")
            P.run("vector", lambda e, sl=sl: e.bn_aggr(
                mv[:, sl * 2:sl * 2 + 2], bns[:, sl * 12:sl * 12 + 12]))
            dv_mv = P.marker("vector", "dv")
            P.wait("scalar", "dv", dv_mv)
            P.run("scalar", lambda e, sl=sl: e.activation(
                out=lnst[:, sl * 4:sl * 4 + 1],
                in_=mv[:, sl * 2 + 1:sl * 2 + 2],
                func=AF.Sqrt, bias=epsc[:], scale=1.0))
            ac_sq = P.marker("scalar", "ac")
            P.wait("vector", "ac", ac_sq)
            P.run("vector", lambda e, sl=sl: e.reciprocal(
                out=lnst[:, sl * 4 + 1:sl * 4 + 2],
                in_=lnst[:, sl * 4:sl * 4 + 1]))
            P.drain("vector")
            P.run("vector", lambda e, sl=sl: e.scalar_tensor_tensor(
                out=lnst[:, sl * 4 + 2:sl * 4 + 3],
                in0=mv[:, sl * 2:sl * 2 + 1], scalar=-1.0,
                in1=lnst[:, sl * 4 + 1:sl * 4 + 2],
                op0=ALU.mult, op1=ALU.mult))
            dv_ln = P.marker("vector", "dv")
            P.wait("scalar", "dv", dv_ln)
            if ch >= 1:
                P.waitp("scalar", xn_spilled[ch - 1])
            ac_xn = P.run("scalar", lambda e, sl=sl: e.activation(
                out=xnf32[:, 0:D],
                in_=x_sb[:, sl * D:(sl + 1) * D],
                func=AF.Identity, bias=lnst[:, sl * 4 + 2:sl * 4 + 3],
                scale=lnst[:, sl * 4 + 1:sl * 4 + 2]), s="ac")
            P.wait("vector", "ac", ac_xn)
            if ch >= 1:
                P.waitp("vector", xn_spilled[ch - 1])
            xnb_done = P.run("vector", lambda e: e.tensor_copy(
                out=xn_sb[:, 0:D], in_=xnf32[:, 0:D]), s="dv")
            P.drain("vector")
            P.wait("scalar", "dv", xnb_done)
            xn_spilled[ch] = ("xnsp0",
                              P.dma("scalar", "xnsp0",
                                    xnsp[row0:row0 + 128, :],
                                    xn_sb[:, 0:D]))
            P.run("vector", lambda e: e.tensor_reduce(
                out=qbmB[:, 0:64],
                in_=xnf32[:, 0:D].rearrange("p (b k) -> p b k", k=16),
                axis=AX.X, op=ALU.max, apply_absolute_value=True))
            P.drain("vector")
            P.run("vector", lambda e: e.tensor_scalar(
                out=qsB[:, 0:64], in0=qbmB[:, 0:64],
                scalar1=float(np.float32(1.0 / 6.0)), scalar2=1e-12,
                op0=ALU.mult, op1=ALU.max))
            P.drain("vector")
            P.run("vector", lambda e: e.reciprocal(
                out=qinvB[:, 0:64], in_=qsB[:, 0:64]))
            P.drain("vector")
            P.run("vector", lambda e: e.tensor_scalar(
                out=qhalfB[:, 0:64], in0=qsB[:, 0:64], scalar1=0.5,
                scalar2=None, op0=ALU.mult))
            P.drain("vector")
            P.run("vector", lambda e, sl=sl: e.tensor_tensor(
                out=x_sb[:, sl * D:(sl + 1) * D]
                    .rearrange("p (b k) -> p b k", k=16),
                in0=xnf32[:, 0:D].rearrange("p (b k) -> p b k", k=16),
                in1=qinvB[:, 0:64].unsqueeze(2).broadcast_to([128, 64, 16]),
                op=ALU.mult))
            P.drain("vector")
            P.run("vector", lambda e, sl=sl: e._custom_dve(
                FP4A, out=qaB[:, 0:D], in0=x_sb[:, sl * D:(sl + 1) * D],
                in1=c3c[:], s0=0.0625, s1=0.5625, imm2=1.5625))
            P.drain("vector")
            P.run("vector", lambda e, sl=sl: e._custom_dve(
                FP4B, out=qaB[:, 0:D], in0=x_sb[:, sl * D:(sl + 1) * D],
                in1=qaB[:, 0:D], s0=6.25, s1=12.25))
            P.drain("vector")
            P.run("vector", lambda e, sl=sl: e._custom_dve(
                FP4C, out=qaB[:, 0:D], in0=qaB[:, 0:D],
                in1=x_sb[:, sl * D:(sl + 1) * D], s0=25.0))
            P.drain("vector")
            if ch >= 1:
                P.wait("vector", "xqsp", xq_spilled[ch - 1])
            xqB_done = P.run("vector", lambda e: e.tensor_tensor(
                out=xqB[:, 0:D].rearrange("p (b k) -> p b k", k=16),
                in0=qaB[:, 0:D].rearrange("p (b k) -> p b k", k=16),
                in1=qhalfB[:, 0:64].unsqueeze(2).broadcast_to([128, 64, 16]),
                op=ALU.mult), s="dv")
            P.drain("vector")
            P.wait("scalar", "dv", xqB_done)
            xq_spilled[ch] = P.dma("scalar", "xqsp", xqsp[row0:row0 + 128, :],
                                   xqB[:, 0:D])

        if t >= 1:
            P.wait("sync", "pe", proj_pe_end[t - 1])
            P.wait("sync", "dv", prev_pool[0])
        xl_sq = 0
        for c in range(8):
            P.waitp("sync", xn_spilled[t * 4 + 3])
            xl_sq = P.dma("sync", "xfq0", xnfm[:, c * TILE:(c + 1) * TILE],
                          xnsp[t * TILE:(t + 1) * TILE,
                               c * 128:(c + 1) * 128],
                          transpose=True)

        if t == 0:
            P.waitp("tensor", w_attn_loaded)
        P.wait("tensor", "xfq0", xl_sq)
        P.wait("tensor", "dv", prev_comb_ev[0])

        def proj_ws(wsb, n_out, noc, dst, dst_scale, evac_eng, evac_sem,
                    dstc0=0, dst_stride=TILE):
            P.row("F")
            P.wait("tensor", "ac", P.val("ac"))
            P.wait("tensor", "dv", P.val("dv"))
            tails = []
            for oc in range(noc):
                bk = oc % 2
                if len(tails) >= 2:
                    P.wait("tensor", evac_sem, tails[-2])
                for k in range(8):
                    P.run("tensor", lambda e, oc=oc, k=k, bk=bk: e.matmul(
                        bank(bk),
                        wsb[:, k * n_out + oc * 128:k * n_out + oc * 128 + 128],
                        xnfm[:, k * TILE:(k + 1) * TILE],
                        start=(k == 0), stop=(k == 7)),
                        s="pe" if k == 7 else None)
                pe_t = P.val("pe")
                P.wait(evac_eng, "pe", pe_t)
                c0 = dstc0 + oc * dst_stride
                if evac_eng == "scalar":
                    tails.append(P.run("scalar", lambda e, c0=c0, bk=bk:
                        e.activation(out=dst[:, c0:c0 + TILE], in_=bank(bk),
                                     func=AF.Copy, bias=0.0, scale=dst_scale),
                        s=evac_sem))
                else:
                    tails.append(P.run("vector", lambda e, c0=c0, bk=bk:
                        e.tensor_copy(out=dst[:, c0:c0 + TILE], in_=bank(bk)),
                        s=evac_sem))
            return tails[-1]

        q_done = proj_ws(wq_s, D, 8, q_sb, 0.25, "scalar", "ac")
        k_done = proj_ws(wk_s, D, 8, kring, 1.0, "vector", "dv",
                         dstc0=par * 512, dst_stride=1024)

        v_done = 0
        P.row("F")
        for j in range(4):
            ch = t * 4 + j
            sl = vslot(ch)
            for s in range(2):
                for k in range(8):
                    P.run("tensor", lambda e, j=j, s=s, k=k: e.matmul(
                        bank(2 + s),
                        xnfm[:, k * TILE + j * 128:k * TILE + (j + 1) * 128],
                        wv_s[:, k * D + s * 512:k * D + (s + 1) * 512],
                        start=(k == 0), stop=(k == 7)),
                        s="pe" if k == 7 else None)
            pe_v = P.val("pe")
            P.wait("vector", "pe", pe_v)
            v_done = P.run("vector", lambda e, sl=sl: e.tensor_copy(
                out=vring[:, sl * H * 65:(sl + 1) * H * 65]
                    .rearrange("p (h c) -> p h c", c=65)[:, :, 0:64],
                in_=bank(2, 1024).rearrange("p (h c) -> p h c", c=64)),
                s="dv")
            P.wait("tensor", "dv", v_done)
        gq_done = proj_ws(wgq_s, GD, 4, gq_sb, 0.25, "scalar", "ac")
        proj_pe_end[t] = P.val("pe")

        P.wait("vector", "xfq0", xl_sq)
        pool_done = 0
        for c in range(8):
            P.run("vector", lambda e, c=c: e.tensor_reduce(
                out=p32[:, c * 8:(c + 1) * 8],
                in_=xnfm[:, c * TILE:(c + 1) * TILE]
                    .rearrange("p (l k) -> p l k", k=64),
                axis=AX.X, op=ALU.add))
        P.drain("vector")
        pool_done = P.run("vector", lambda e, sq_i=sq_i: e.tensor_copy(
            out=p_sb[:].rearrange("p (c l) -> p c l", l=LATB)
                [:, :, sq_i * 8:sq_i * 8 + 8],
            in_=p32[:].rearrange("p (c l) -> p c l", l=8)), s="dv")
        P.drain("vector")
        prev_pool[0] = pool_done

        P.wait("tensor", "dv", pool_done)
        gkgv_done = 0
        P.row("F")
        for (wsb, dst) in ((wgk_s, gk_sb), (wgv_s, gv_sb)):
            for oc in range(4):
                for k in range(8):
                    P.run("tensor", lambda e, wsb=wsb, oc=oc, k=k,
                          sq_i=sq_i: e.matmul(
                        bank(6, 8),
                        wsb[:, k * GD + oc * 128:k * GD + oc * 128 + 128],
                        p_sb[:, k * LATB + sq_i * 8:k * LATB + sq_i * 8 + 8],
                        start=(k == 0), stop=(k == 7)),
                        s="pe" if k == 7 else None)
                pe_g = P.val("pe")
                P.wait("scalar", "pe", pe_g)
                gkgv_done = P.run("scalar", lambda e, oc=oc, dst=dst,
                                  sq_i=sq_i:
                    e.activation(
                        out=dst[:, oc * LATB + sq_i * 8:oc * LATB + sq_i * 8 + 8],
                        in_=bank(6, 8), func=AF.Copy, bias=0.0,
                        scale=1.0 / 64.0), s="ac")
                P.wait("tensor", "ac", gkgv_done)

        gvp_done = 0
        LW = min(LATB, 64)
        for pb_i in range(4):
            for hh in range(2):
                P.row(hh * 64)
                for s in range(2):
                    P.run("tensor", lambda e, pb_i=pb_i, s=s, hh=hh,
                          LW=LW: e.matmul(
                        bank(6 + s)[hh * 64:hh * 64 + LW, :],
                        gv_sb[hh * 64:(hh + 1) * 64,
                              pb_i * LATB:pb_i * LATB + LW],
                        rout_s[hh * 64:(hh + 1) * 64,
                               pb_i * D + s * 512:pb_i * D + (s + 1) * 512],
                        start=True, stop=True,
                        tile_position=(hh * 64, hh * 64)),
                        s="pe" if (s == 1 and hh == 1) else None)
            pe_gvp = P.val("pe")
            P.wait("vector", "pe", pe_gvp)
            if LW == 64:
                gvp_done = P.run("vector", lambda e, pb_i=pb_i: e.tensor_copy(
                    out=gvp_sb[:, pb_i * D:(pb_i + 1) * D], in_=bank(6, 1024)),
                    s="dv")
            else:
                for hh in range(2):
                    gvp_done = P.run("vector",
                        lambda e, pb_i=pb_i, hh=hh, LW=LW: e.tensor_copy(
                            out=gvp_sb[hh * 64:hh * 64 + LW,
                                       pb_i * D:(pb_i + 1) * D],
                            in_=bank(6, 1024)[hh * 64:hh * 64 + LW, :]),
                        s="dv")
                P.drain("vector")
            P.wait("tensor", "dv", gvp_done)

        P.row("F")
        for k in range(8):
            P.run("tensor", lambda e, k=k: e.matmul(
                bank(6)[0:32, :], gdiag_s[:, k * 32:(k + 1) * 32],
                q_sb[:, k * TILE:(k + 1) * TILE],
                start=(k == 0), stop=(k == 7)), s="pe" if k == 7 else None)
        pe_gt = P.val("pe")
        P.wait("scalar", "pe", pe_gt)
        P.wait("scalar", "gtq0", prev_gt_sq[0])
        g_sig = P.run("scalar", lambda e: e.activation(
            out=gates_pre[:], in_=bank(6)[0:32, :],
            func=AF.Sigmoid, bias=0.0, scale=4.0), s="ac")
        gt_sq = 0
        for jj in range(4):
            P.wait("sync", "ac", g_sig)
            if t >= 2:
                P.wait("sync", "dv", att_dv_end[t - 2])
            gt_sq = P.dma("sync", "gtq0",
                          gates_tm[:, par * 128 + jj * 32:
                                   par * 128 + (jj + 1) * 32],
                          gates_pre[0:32, jj * 128:(jj + 1) * 128],
                          transpose=True)
        prev_gt_sq[0] = gt_sq

        if ABL == 1:
            for jj in range(4):
                qi = t * 4 + jj
                qp = qi % 2
                P.wait("vector", "pe", P.val("pe"))
                if qi >= 2:
                    P.waitp("vector", comb_stored[qi - 2])
                comb_ev = P.run("vector", lambda e, qp=qp: e.memset(
                    comb_sb[:, qp * D:(qp + 1) * D], 0.0), s="dv")
                P.drain("vector")
                prev_comb_ev[0] = comb_ev
                P.wait("scalar", "dv", comb_ev)
                comb_stored[qi] = (f"cb{qp}",
                                   P.dma("scalar", f"cb{qp}",
                                         combsp[qi * 128:(qi + 1) * 128, :],
                                         comb_sb[:, qp * D:(qp + 1) * D]))
            att_dv_end[t] = P.val("dv")
            continue
        for jj in range(4):
            qi = t * 4 + jj
            sqj = sq_i * 4 + jj
            qp = qi % 2
            self_ch = qi
            if sqj == 0:
                kcs = [(self_ch, 0)]
                wid = 128
            elif sqj == 1:
                kcs = [(self_ch, 0), (self_ch - 1, 1)]
                wid = 256
            else:
                kcs = [(self_ch - 2, 0), (self_ch, 1), (self_ch - 1, 2)]
                wid = 384

            mask_dv_bp[0] = max(mask_dv_bp[0], prev_m1[0])
            mask_dv_bp[1] = max(mask_dv_bp[1], prev_m2[0])
            for p_i in range(8):
                hA = 2 * p_i
                bp = 4 * (p_i % 2)
                P.wait("tensor", "dv", mask_dv_bp[p_i % 2])
                for hh in range(2):
                    h = hA + hh
                    P.row((h % 2) * 64)
                    for ci, (chk, role) in enumerate(kcs):
                        half = ((chk // 4) % 2) * 512
                        koff = (chk % 4) * 128
                        P.run("tensor", lambda e, chk=chk, role=role, hh=hh,
                              h=h, half=half, koff=koff, ci=ci, jj=jj, bp=bp,
                              nk=len(kcs): e.matmul(
                            bank(bp + hh)[:, role * 128:(role + 1) * 128],
                            kring[(h % 2) * 64:(h % 2) * 64 + 64,
                                  (h // 2) * 1024 + half + koff:
                                  (h // 2) * 1024 + half + koff + 128],
                            q_sb[(h % 2) * 64:(h % 2) * 64 + 64,
                                 (h // 2) * TILE + jj * 128:
                                 (h // 2) * TILE + (jj + 1) * 128],
                            start=(ci == 0), stop=(ci == nk - 1),
                            tile_position=((h % 2) * 64, 0)),
                            s="pe" if (ci == len(kcs) - 1 and hh == 1) else None)
                pe_sc = P.val("pe")
                P.wait("vector", "pe", pe_sc)
                if p_i == 0:
                    P.wait("vector", "ac", prev_exp[0])
                for hh in range(2):
                    h = hA + hh
                    if sqj == 0:
                        stt, end = iota_p[:], cend[:, 0:1]
                    elif sqj == 1:
                        stt, end = iota_p[:], cend[:, 1:2]
                    else:
                        stt, end = iota_p1[:], iota_p[:]
                    P.run("vector",
                        lambda e, hh=hh, h=h, stt=stt, end=end, bp=bp,
                               w=wid: e.tensor_mask_reduce(
                            out=msk_sb[:, h * 384:h * 384 + w],
                            in_=bank(bp + hh)[:, 0:w],
                            mask_start=stt, mask_end=end, scale=0.5,
                            accum_in=-1e30,
                            op=ALU.max, accum_out=cacc[:, 4 + h:5 + h]),
                        s="dv")
                mask_dv_bp[p_i % 2] = P.val("dv")
            P.drain("vector")
            P.run("vector", lambda e: e.tensor_reduce(
                out=cacc[:, 0:1], in_=cacc[:, 4:20],
                axis=AX.X, op=ALU.max, apply_absolute_value=False), s="dv")
            P.drain("vector")
            masks_done = P.val("dv")
            P.wait("gpsimd", "dv", masks_done)
            gp_c = P.run("gpsimd", lambda e: e.partition_all_reduce(
                cacc[:, 1:2], cacc[:, 0:1], channels=128,
                reduce_op=ReduceOp.max), s="gp")
            P.wait("vector", "gp", gp_c)
            P.run("vector", lambda e: e.tensor_scalar(
                out=cacc[:, 2:3], in0=cacc[:, 1:2], scalar1=-1.0,
                scalar2=None, op0=ALU.mult))
            negc_done = P.marker("vector", "dv")

            P.wait("scalar", "dv", negc_done)
            P.wait("scalar", "pe", prev_pv[qp])
            exp_done = 0
            for h in range(H):
                exp_done = P.run("scalar", lambda e, h=h, w=wid,
                                 qp=qp: e.activation(
                    out=pt_sb[:, qp * H * 384 + h * 384:
                              qp * H * 384 + h * 384 + w],
                    in_=msk_sb[:, h * 384:h * 384 + w],
                    func=AF.Exp, bias=cacc[:, 2:3], scale=1.0), s="ac")
            prev_exp[0] = exp_done

            if ABL == 3:
                P.wait("vector", "ac", exp_done)
                P.wait("vector", "pe", P.val("pe"))
                if qi >= 2:
                    P.waitp("vector", comb_stored[qi - 2])
                comb_ev = P.run("vector", lambda e, qp=qp: e.memset(
                    comb_sb[:, qp * D:(qp + 1) * D], 0.0), s="dv")
                P.drain("vector")
                prev_comb_ev[0] = comb_ev
                prev_pv[qp] = P.val("pe")
                P.wait("scalar", "dv", comb_ev)
                comb_stored[qi] = (f"cb{qp}",
                                   P.dma("scalar", f"cb{qp}",
                                         combsp[qi * 128:(qi + 1) * 128, :],
                                         comb_sb[:, qp * D:(qp + 1) * D]))
                continue
            P.wait("tensor", "ac", exp_done)
            P.wait("tensor", "dv", v_done)
            nkc = len(kcs)
            pv_done = 0
            P.row("F")
            for h in range(H):
                for ci, (chk, role) in enumerate(kcs):
                    pv_done = P.run("tensor",
                        lambda e, h=h, chk=chk, role=role, ci=ci, qp=qp,
                               nkc=nkc: e.matmul(
                            bank(2 + h // 4)[:, (h % 4) * 65:(h % 4) * 65 + 65],
                            pt_sb[:, qp * H * 384 + h * 384 + role * 128:
                                  qp * H * 384 + h * 384 + (role + 1) * 128],
                            vring[:, vslot(chk) * H * 65 + h * 65:
                                  vslot(chk) * H * 65 + (h + 1) * 65],
                            start=(h % 4 == 0 and ci == 0),
                            stop=(h % 4 == 3 and ci == nkc - 1)),
                        s="pe" if (h == H - 1 and ci == nkc - 1) else None)
            prev_pv[qp] = pv_done

            if ABL == 4:
                P.wait("vector", "pe", pv_done)
                if qi >= 2:
                    P.waitp("vector", comb_stored[qi - 2])
                comb_ev = P.run("vector", lambda e, qp=qp: e.memset(
                    comb_sb[:, qp * D:(qp + 1) * D], 0.0), s="dv")
                P.drain("vector")
                prev_comb_ev[0] = comb_ev
                P.wait("scalar", "dv", comb_ev)
                comb_stored[qi] = (f"cb{qp}",
                                   P.dma("scalar", f"cb{qp}",
                                         combsp[qi * 128:(qi + 1) * 128, :],
                                         comb_sb[:, qp * D:(qp + 1) * D]))
                continue
            P.wait("tensor", "ac", g_sig)
            P.wait("tensor", "dv", prev_m3[0])
            for hh in range(2):
                P.row(hh * 64)
                for ci, h in enumerate((hh, 2 + hh, 4 + hh, 6 + hh)):
                    P.run("tensor", lambda e, h=h, jj=jj, ci=ci,
                          hh=hh: e.matmul(
                        bank(6 + hh)[:, (h // 2) * 64:(h // 2) * 64 + LATB],
                        gq_sb[hh * 64:hh * 64 + 64,
                              (h // 2) * TILE + jj * 128:
                              (h // 2) * TILE + (jj + 1) * 128],
                        gk_sb[hh * 64:hh * 64 + 64,
                              (h // 2) * LATB:(h // 2) * LATB + LATB],
                        start=(ci == 0), stop=(ci == 3),
                        tile_position=(hh * 64, 0)),
                        s="pe" if (hh == 1 and ci == 3) else None)
            pe_rs = P.val("pe")
            if ABL in (9, 10):
                P.wait("vector", "pe", pe_rs)
                if qi >= 2:
                    P.waitp("vector", comb_stored[qi - 2])
                comb_ev = P.run("vector", lambda e, qp=qp: e.memset(
                    comb_sb[:, qp * D:(qp + 1) * D], 0.0), s="dv")
                P.drain("vector")
                prev_comb_ev[0] = comb_ev
                P.wait("scalar", "dv", comb_ev)
                comb_stored[qi] = (f"cb{qp}",
                                   P.dma("scalar", f"cb{qp}",
                                         combsp[qi * 128:(qi + 1) * 128, :],
                                         comb_sb[:, qp * D:(qp + 1) * D]))
                continue
            P.wait("vector", "pe", pe_rs)
            for h in range(LH):
                P.run("vector", lambda e, h=h, sq_i=sq_i,
                      jj=jj: e.tensor_mask_reduce(
                    out=bank(6 + h % 2)[:, (h // 2) * 64:(h // 2) * 64 + LATB],
                    in_=bank(6 + h % 2)[:, (h // 2) * 64:(h // 2) * 64 + LATB],
                    mask_start=0.0,
                    mask_end=ends_r[:, sq_i * 4 + jj:sq_i * 4 + jj + 1],
                    scale=0.5, accum_in=-1e30,
                    op=ALU.max, accum_out=cacc[:, 4 + h:5 + h]), s="dv")
            P.drain("vector")
            rmask_done = P.run("vector", lambda e: e.tensor_reduce(
                out=cacc[:, 0:1], in_=cacc[:, 4:12],
                axis=AX.X, op=ALU.max, apply_absolute_value=False), s="dv")
            P.drain("vector")
            if ABL == 8:
                P.wait("vector", "dv", rmask_done)
                if qi >= 2:
                    P.waitp("vector", comb_stored[qi - 2])
                comb_ev = P.run("vector", lambda e, qp=qp: e.memset(
                    comb_sb[:, qp * D:(qp + 1) * D], 0.0), s="dv")
                P.drain("vector")
                prev_comb_ev[0] = comb_ev
                P.wait("scalar", "dv", comb_ev)
                comb_stored[qi] = (f"cb{qp}",
                                   P.dma("scalar", f"cb{qp}",
                                         combsp[qi * 128:(qi + 1) * 128, :],
                                         comb_sb[:, qp * D:(qp + 1) * D]))
                continue
            P.wait("gpsimd", "dv", rmask_done)
            gp_cr = P.run("gpsimd", lambda e: e.partition_all_reduce(
                cacc[:, 1:2], cacc[:, 0:1], channels=128,
                reduce_op=ReduceOp.max), s="gp")
            P.wait("vector", "gp", gp_cr)
            P.run("vector", lambda e: e.tensor_scalar(
                out=cacc[:, 3:4], in0=cacc[:, 1:2], scalar1=-1.0,
                scalar2=None, op0=ALU.mult))
            negcr_done = P.marker("vector", "dv")
            P.wait("scalar", "dv", negcr_done)
            P.wait("scalar", "pe", prev_prtrans[qp])
            rexp_done = 0
            for h in range(LH):
                rexp_done = P.run("scalar", lambda e, h=h, qp=qp: e.activation(
                    out=prem[:, qp * LH * 64 + h * 64:
                             qp * LH * 64 + h * 64 + LATB],
                    in_=bank(6 + h % 2)[:, (h // 2) * 64:
                                        (h // 2) * 64 + LATB],
                    func=AF.Exp, bias=cacc[:, 3:4], scale=1.0,
                    accum_out=rsum_r[:, h:h + 1]), s="ac")
            P.wait("vector", "ac", rexp_done)
            P.run("vector", lambda e: e.reciprocal(
                out=csc[:, 8:16], in_=rsum_r[:, 0:8]))
            P.drain("vector")
            prn_done = 0
            for h in range(LH):
                prn_done = P.run("vector", lambda e, h=h, qp=qp: e.tensor_scalar(
                    out=prem[:, qp * LH * 64 + h * 64:
                             qp * LH * 64 + h * 64 + LATB],
                    in0=prem[:, qp * LH * 64 + h * 64:
                             qp * LH * 64 + h * 64 + LATB],
                    scalar1=csc[:, 8 + h:9 + h], scalar2=None, op0=ALU.mult),
                    s="dv")
                P.drain("vector")
            if ABL == 6:
                P.wait("vector", "dv", prn_done)
                if qi >= 2:
                    P.waitp("vector", comb_stored[qi - 2])
                comb_ev = P.run("vector", lambda e, qp=qp: e.memset(
                    comb_sb[:, qp * D:(qp + 1) * D], 0.0), s="dv")
                P.drain("vector")
                prev_comb_ev[0] = comb_ev
                prev_prtrans[qp] = P.val("pe")
                P.wait("scalar", "dv", comb_ev)
                comb_stored[qi] = (f"cb{qp}",
                                   P.dma("scalar", f"cb{qp}",
                                         combsp[qi * 128:(qi + 1) * 128, :],
                                         comb_sb[:, qp * D:(qp + 1) * D]))
                continue
            P.wait("tensor", "dv", prn_done)
            prt_done = 0
            prt_copies = [0, 0, 0, 0]
            for pb_i in range(4):
                tb = pb_i % 2
                if pb_i >= 2:
                    P.wait("tensor", "dv", prt_copies[pb_i - 2])
                P.row("F")
                tr = P.run("tensor", lambda e, pb_i=pb_i, qp=qp,
                           tb=tb: e.transpose(
                    bank(tb, 64).bitcast(BF16),
                    prem[:, qp * LH * 64 + pb_i * 128:
                         qp * LH * 64 + (pb_i + 1) * 128],
                    ident[:]), s="pe")
                P.wait("vector", "pe", tr)
                prt_done = P.run("vector", lambda e, pb_i=pb_i,
                                 qp=qp, tb=tb: e.tensor_copy(
                    out=prt_sb[:, qp * 512 + pb_i * 128:
                               qp * 512 + (pb_i + 1) * 128],
                    in_=bank(tb, 64).bitcast(BF16)), s="dv")
                prt_copies[pb_i] = prt_done
            P.wait("tensor", "dv", prt_done)
            prev_prtrans[qp] = P.val("pe")
            if ABL == 7:
                P.wait("vector", "dv", prt_done)
                if qi >= 2:
                    P.waitp("vector", comb_stored[qi - 2])
                comb_ev = P.run("vector", lambda e, qp=qp: e.memset(
                    comb_sb[:, qp * D:(qp + 1) * D], 0.0), s="dv")
                P.drain("vector")
                prev_comb_ev[0] = comb_ev
                P.wait("scalar", "dv", comb_ev)
                comb_stored[qi] = (f"cb{qp}",
                                   P.dma("scalar", f"cb{qp}",
                                         combsp[qi * 128:(qi + 1) * 128, :],
                                         comb_sb[:, qp * D:(qp + 1) * D]))
                continue
            P.wait("tensor", "ac", rexp_done)
            for par_h in range(2):
                P.row(par_h * 64)
                b0 = 6 if par_h == 0 else 0
                for s in range(2):
                    for oi, h in enumerate((par_h, 2 + par_h, 4 + par_h,
                                            6 + par_h)):
                        P.run("tensor", lambda e, s=s, h=h, qp=qp, oi=oi,
                              b0=b0: e.matmul(
                            bank(b0, 1024)[:, s * 512:(s + 1) * 512],
                            prt_sb[(h % 2) * 64:(h % 2) * 64 + 64,
                                   qp * 512 + (h // 2) * 128:
                                   qp * 512 + (h // 2) * 128 + 128],
                            gvp_sb[(h % 2) * 64:(h % 2) * 64 + 64,
                                   (h // 2) * D + s * 512:
                                   (h // 2) * D + (s + 1) * 512],
                            start=(oi == 0), stop=(oi == 3),
                            tile_position=((h % 2) * 64, 0)),
                            s="pe" if (oi == 3 and s == 1 and par_h == 1)
                              else None)
            rpv_done = P.val("pe")

            if ABL == 5:
                P.wait("vector", "pe", rpv_done)
                if qi >= 2:
                    P.waitp("vector", comb_stored[qi - 2])
                comb_ev = P.run("vector", lambda e, qp=qp: e.memset(
                    comb_sb[:, qp * D:(qp + 1) * D], 0.0), s="dv")
                P.drain("vector")
                prev_comb_ev[0] = comb_ev
                P.wait("scalar", "dv", comb_ev)
                comb_stored[qi] = (f"cb{qp}",
                                   P.dma("scalar", f"cb{qp}",
                                         combsp[qi * 128:(qi + 1) * 128, :],
                                         comb_sb[:, qp * D:(qp + 1) * D]))
                continue
            P.wait("vector", "pe", rpv_done)
            P.wait("vector", "gtq0", gt_sq)
            if qi >= 2:
                P.waitp("vector", comb_stored[qi - 2])
            P.run("vector", lambda e, qp=qp, par=par, jj=jj: e.tensor_tensor(
                out=rem_sb[:, qp * D:(qp + 1) * D]
                    .rearrange("p (h d) -> p h d", d=64),
                in0=bank(0, 1024).rearrange("p (h d) -> p h d", d=64),
                in1=gates_tm[:, par * 128 + jj * 32 + 16:
                             par * 128 + jj * 32 + 32]
                    .unsqueeze(2).broadcast_to([128, 16, 64]),
                op=ALU.mult), s="dv")
            prev_m1[0] = P.val("dv")
            for b in range(4):
                P.run("vector", lambda e, b=b: e.tensor_copy(
                    out=csc[:, 16 + b * 4:16 + (b + 1) * 4],
                    in_=bank(2 + b)[:, 0:260]
                        .rearrange("p (h c) -> p h c", c=65)[:, :, 64:65]
                        .squeeze(2)))
            P.drain("vector")
            P.run("vector", lambda e: e.reciprocal(out=csc[:, 0:8],
                                                   in_=csc[:, 16:24]))
            P.drain("vector")
            P.run("vector", lambda e: e.reciprocal(out=csc[:, 32:40],
                                                   in_=csc[:, 24:32]))
            P.drain("vector")
            P.run("vector", lambda e, par=par, jj=jj: e.tensor_tensor(
                out=c_l[:, 0:8], in0=csc[:, 0:8],
                in1=gates_tm[:, par * 128 + jj * 32:par * 128 + jj * 32 + 8],
                op=ALU.mult))
            P.run("vector", lambda e, par=par, jj=jj: e.tensor_tensor(
                out=c_l[:, 8:16], in0=csc[:, 32:40],
                in1=gates_tm[:, par * 128 + jj * 32 + 8:
                             par * 128 + jj * 32 + 16],
                op=ALU.mult))
            P.drain("vector")
            for b in range(4):
                P.run("vector", lambda e, b=b, qp=qp: e.tensor_tensor(
                    out=comb_sb[:, qp * D + b * 256:qp * D + (b + 1) * 256]
                        .rearrange("p (h d) -> p h d", d=64),
                    in0=bank(2 + b)[:, 0:260]
                        .rearrange("p (h c) -> p h c", c=65)[:, :, 0:64],
                    in1=c_l[:, b * 4:(b + 1) * 4].unsqueeze(2)
                        .broadcast_to([128, 4, 64]),
                    op=ALU.mult), s="dv")
            prev_m2[0] = P.val("dv")
            P.drain("vector")
            P.run("vector", lambda e, qp=qp: e.tensor_tensor(
                out=comb_sb[:, qp * D:(qp + 1) * D],
                in0=comb_sb[:, qp * D:(qp + 1) * D],
                in1=rem_sb[:, qp * D:(qp + 1) * D], op=ALU.add))
            P.drain("vector")
            P.run("vector", lambda e, qp=qp, par=par, jj=jj: e.tensor_tensor(
                out=rem_sb[:, qp * D:(qp + 1) * D]
                    .rearrange("p (h d) -> p h d", d=64),
                in0=bank(6, 1024).rearrange("p (h d) -> p h d", d=64),
                in1=gates_tm[:, par * 128 + jj * 32 + 16:
                             par * 128 + jj * 32 + 32]
                    .unsqueeze(2).broadcast_to([128, 16, 64]),
                op=ALU.mult), s="dv")
            prev_m3[0] = P.val("dv")
            P.drain("vector")
            comb_ev = P.run("vector", lambda e, qp=qp: e.tensor_tensor(
                out=comb_sb[:, qp * D:(qp + 1) * D],
                in0=comb_sb[:, qp * D:(qp + 1) * D],
                in1=rem_sb[:, qp * D:(qp + 1) * D], op=ALU.add), s="dv")
            P.drain("vector")
            prev_comb_ev[0] = comb_ev
            P.wait("scalar", "dv", comb_ev)
            comb_stored[qi] = (f"cb{qp}",
                               P.dma("scalar", f"cb{qp}",
                                     combsp[qi * 128:(qi + 1) * 128, :],
                                     comb_sb[:, qp * D:(qp + 1) * D]))
        att_dv_end[t] = P.val("dv")

    pe_endB = P.marker("tensor", "pe")
    ac_endB = P.marker("scalar", "ac")
    dv_endB = P.marker("vector", "dv")
    for eng in ("vector", "gpsimd", "sync", "scalar", "tensor"):
        P.wait(eng, "pe", pe_endB)
        P.wait(eng, "ac", ac_endB)
        P.wait(eng, "dv", dv_endB)
        P.waitp(eng, comb_stored[NC_TOK - 1])
        P.waitp(eng, comb_stored[NC_TOK - 2])
        P.wait(eng, "xqsp", xq_spilled[NC_TOK - 1])
        P.waitp(eng, xn_spilled[NC_TOK - 1])

    if ABL in (1, 2) and int(os.environ.get("K_ABL_PHC", "1")):
        out_stored = [0] * NC_TOK
        for ch in range(NC_TOK):
            row0 = ch * 128
            P.run("vector", lambda e: e.memset(out_sb[:, 0:D], 0.0))
            P.drain("vector")
            od = P.marker("vector", "dv")
            P.wait("scalar", "dv", od)
            out_stored[ch] = P.dma("scalar", "outst", y_h[row0:row0 + 128, :],
                                   out_sb[:, 0:D])
        P.wait("gpsimd", "outst", out_stored[NC_TOK - 1])
        with nc.Block() as block:
            @block.sync
            def _(e):
                for fn in P.ops["sync"]:
                    fn(e)

            @block.scalar
            def _(e):
                for fn in P.ops["scalar"]:
                    fn(e)

            @block.vector
            def _(e):
                for fn in P.ops["vector"]:
                    fn(e)

            @block.tensor
            def _(e):
                for fn in P.ops["tensor"]:
                    fn(e)

            @block.gpsimd
            def _(e):
                for fn in P.ops["gpsimd"]:
                    fn(e)
        return nc

    def wprep(src_h, nat_h, nrow, ncol):
        nch = nrow // 128
        nb = ncol // 16
        last = 0
        prev_store = P.val("wpst")
        for c in range(nch):
            ld = P.dma("gpsimd", "wpl", wqf[:, 0:ncol],
                       src_h[c * 128:(c + 1) * 128, :])
            P.wait("vector", "wpl", ld)
            P.run("vector", lambda e, nb=nb, ncol=ncol: e.tensor_reduce(
                out=wqbm[:, 0:nb],
                in_=wqf[:, 0:ncol].rearrange("p (b k) -> p b k", k=16),
                axis=AX.X, op=ALU.max, apply_absolute_value=True))
            P.drain("vector")
            P.run("vector", lambda e, nb=nb: e.tensor_scalar(
                out=wqs[:, 0:nb], in0=wqbm[:, 0:nb],
                scalar1=float(np.float32(1.0 / 6.0)), scalar2=1e-12,
                op0=ALU.mult, op1=ALU.max))
            P.drain("vector")
            P.run("vector", lambda e, nb=nb: e.reciprocal(
                out=wqinv[:, 0:nb], in_=wqs[:, 0:nb]))
            P.drain("vector")
            P.run("vector", lambda e, nb=nb: e.tensor_scalar(
                out=wqhalf[:, 0:nb], in0=wqs[:, 0:nb], scalar1=0.5,
                scalar2=None, op0=ALU.mult))
            P.drain("vector")
            P.run("vector", lambda e, nb=nb, ncol=ncol: e.tensor_tensor(
                out=wqy[:, 0:ncol].rearrange("p (b k) -> p b k", k=16),
                in0=wqf[:, 0:ncol].rearrange("p (b k) -> p b k", k=16),
                in1=wqinv[:, 0:nb].unsqueeze(2).broadcast_to([128, nb, 16]),
                op=ALU.mult))
            P.drain("vector")
            P.run("vector", lambda e, ncol=ncol: e._custom_dve(
                FP4A, out=wqa[:, 0:ncol], in0=wqy[:, 0:ncol], in1=c3c[:],
                s0=0.0625, s1=0.5625, imm2=1.5625))
            P.drain("vector")
            P.run("vector", lambda e, ncol=ncol: e._custom_dve(
                FP4B, out=wqf[:, 0:ncol], in0=wqy[:, 0:ncol],
                in1=wqa[:, 0:ncol], s0=6.25, s1=12.25))
            P.drain("vector")
            P.run("vector", lambda e, ncol=ncol: e._custom_dve(
                FP4C, out=wqa[:, 0:ncol], in0=wqf[:, 0:ncol],
                in1=wqy[:, 0:ncol], s0=25.0))
            P.drain("vector")
            P.wait("vector", "wpst", prev_store)
            wq_done = P.run("vector", lambda e, nb=nb, ncol=ncol:
                e.tensor_tensor(
                    out=wqout[:, 0:ncol].rearrange("p (b k) -> p b k", k=16),
                    in0=wqa[:, 0:ncol].rearrange("p (b k) -> p b k", k=16),
                    in1=wqhalf[:, 0:nb].unsqueeze(2)
                        .broadcast_to([128, nb, 16]),
                    op=ALU.mult), s="dv")
            P.drain("vector")
            P.wait("gpsimd", "dv", wq_done)
            prev_store = P.dma("gpsimd", "wpst",
                               nat_h[c * 128:(c + 1) * 128, :],
                               wqout[:, 0:ncol])
            last = prev_store
        return last

    wg_nat_done = wprep(wg_h, wgnat, I, D)
    wu_nat_done = wprep(wu_h, wunat, I, D)
    wd_nat_done = wprep(wd_h, wdnat, D, I)
    wprep_all = max(wg_nat_done, wu_nat_done, wd_nat_done)

    wfm_sq = 0
    for (nat, dst, nch, ncol) in ((wgnat, wgq_fm, 8, I), (wunat, wuq_fm, 8, I),
                                  (wdnat, wdq_fm, 22, D)):
        for c in range(nch):
            P.wait("sync", "wpst", wprep_all)
            for r0 in range(0, ncol, 512):
                rw = min(512, ncol - r0)
                wfm_sq = P.dma("sync", "wfmq0",
                               dst[:, c * ncol + r0:c * ncol + r0 + rw],
                               nat[r0:r0 + rw, c * 128:(c + 1) * 128],
                               transpose=True)
    wout_loaded = 0
    for c in range(8):
        wout_loaded = P.dma("gpsimd", "wo2", wout_s2[:, c * D:(c + 1) * D],
                            wout_h[c * 128:(c + 1) * 128, :])

    hq_spilled = [0] * NC_TOK
    out_stored = [0] * NC_TOK
    gu_pe_end = [0] * NT
    dn_pe_end = [0] * NT

    P.wait("tensor", "wfmq0", wfm_sq)
    P.wait("tensor", "wo2", wout_loaded)

    def emit_quant(ncol, nb, xin, xout, yb, ab, bb):
        P.run("vector", lambda e: e.tensor_reduce(
            out=qbm[:, 0:nb], in_=xin.rearrange("p (b k) -> p b k", k=16),
            axis=AX.X, op=ALU.max, apply_absolute_value=True))
        P.drain("vector")
        P.run("vector", lambda e: e.tensor_scalar(
            out=qs[:, 0:nb], in0=qbm[:, 0:nb],
            scalar1=float(np.float32(1.0 / 6.0)), scalar2=1e-12,
            op0=ALU.mult, op1=ALU.max))
        P.drain("vector")
        P.run("vector", lambda e: e.reciprocal(
            out=qbm[:, 0:nb], in_=qs[:, 0:nb]))
        P.drain("vector")
        P.run("vector", lambda e: e.tensor_copy(
            out=qinvb[:, 0:nb], in_=qbm[:, 0:nb]))
        P.run("vector", lambda e: e.tensor_scalar(
            out=qhalf[:, 0:nb], in0=qs[:, 0:nb], scalar1=0.5, scalar2=None,
            op0=ALU.mult))
        P.drain("vector")
        P.run("vector", lambda e: e.tensor_tensor(
            out=yb.rearrange("p (b k) -> p b k", k=16),
            in0=xin.rearrange("p (b k) -> p b k", k=16),
            in1=qinvb[:, 0:nb].unsqueeze(2).broadcast_to([128, nb, 16]),
            op=ALU.mult))
        P.drain("vector")
        P.run("vector", lambda e: e._custom_dve(
            FP4A, out=ab, in0=yb, in1=c3c[:],
            s0=0.0625, s1=0.5625, imm2=1.5625))
        P.drain("vector")
        P.run("vector", lambda e: e._custom_dve(
            FP4B, out=bb, in0=yb, in1=ab, s0=6.25, s1=12.25))
        P.drain("vector")
        P.run("vector", lambda e: e._custom_dve(
            FP4C, out=ab, in0=bb, in1=yb, s0=25.0))
        P.drain("vector")
        tgt = P.run("vector", lambda e: e.tensor_tensor(
            out=xout.rearrange("p (b k) -> p b k", k=16),
            in0=ab.rearrange("p (b k) -> p b k", k=16),
            in1=qhalf[:, 0:nb].unsqueeze(2).broadcast_to([128, nb, 16]),
            op=ALU.mult), s="dv")
        P.drain("vector")
        return tgt

    for t in range(NT):
        if t >= 1:
            P.wait("sync", "pe", dn_pe_end[t - 1])
        xq_sq = cb_sq = 0
        for c in range(8):
            P.wait("sync", "xqsp", xq_spilled[t * 4 + 3])
            xq_sq = P.dma("sync", "xqfq0", xqfm2[:, c * TILE:(c + 1) * TILE],
                          xqsp[t * TILE:(t + 1) * TILE,
                               c * 128:(c + 1) * 128],
                          transpose=True)
            cb_sq = P.dma("sync", "cbfq0", combfm[:, c * TILE:(c + 1) * TILE],
                          combsp[t * TILE:(t + 1) * TILE,
                                 c * 128:(c + 1) * 128],
                          transpose=True)

        P.row("F")
        P.wait("tensor", "xqfq0", xq_sq)
        if _spk == "ExternalOutput" and t == 0:
            P.wait("gpsimd", "xqfq0", xq_sq)
            P.wait("gpsimd", "wfmq0", wfm_sq)
            P.dma("gpsimd", "wo2", wgfmd[:], wgq_fm[:])
            P.dma("gpsimd", "wo2", xqfmd[:], xqfm2[:])
        prev_h = 0
        h_hist = [0, 0]
        for j in range(4):
            ch = t * 4 + j
            row0 = ch * 128
            for s in range(6):
                w = 512 if s < 5 else 256
                col = s * 512
                sl = j * 6 + s
                bb = 2 * (sl % 2)
                P.wait("tensor", "dv", h_hist[sl % 2])
                for k in range(8):
                    P.run("tensor", lambda e, j=j, k=k, col=col, w=w,
                          bb=bb: e.matmul(
                        bank(bb, w),
                        xqfm2[:, k * TILE + j * 128:k * TILE + (j + 1) * 128],
                        wgq_fm[:, k * I + col:k * I + col + w],
                        start=(k == 0), stop=(k == 7)))
                    P.run("tensor", lambda e, j=j, k=k, col=col, w=w,
                          bb=bb: e.matmul(
                        bank(bb + 1, w),
                        xqfm2[:, k * TILE + j * 128:k * TILE + (j + 1) * 128],
                        wuq_fm[:, k * I + col:k * I + col + w],
                        start=(k == 0), stop=(k == 7)),
                        s="pe" if k == 7 else None)
                pe_gu = P.val("pe")
                P.wait("scalar", "pe", pe_gu)
                P.wait("scalar", "dv", prev_h)
                sg_done = P.run("scalar", lambda e, w=w, bb=bb: e.activation(
                    out=sg_sb[:, 0:w], in_=bank(bb, w),
                    func=AF.Silu, bias=0.0, scale=1.0), s="ac")
                P.wait("vector", "ac", sg_done)
                prev_h = P.run("vector", lambda e, col=col, w=w,
                               bb=bb: e.tensor_tensor(
                    out=h_sb[:, col:col + w], in0=bank(bb + 1, w),
                    in1=sg_sb[:, 0:w], op=ALU.mult), s="dv")
                P.drain("vector")
                h_hist[sl % 2] = prev_h
            gu_pe_end[t] = P.val("pe")
            if ch >= 1:
                P.wait("gpsimd", "hqsp", hq_spilled[ch - 1])
                P.wait("vector", "hqsp", hq_spilled[ch - 1])
            if int(os.environ.get("K_SKIP_HQ", "0")):
                P.wait("vector", "dv", prev_h)
                hq_done = P.run("vector", lambda e: e.tensor_copy(
                    out=qy[:, 0:I], in_=h_sb[:, 0:I]), s="dv")
                P.drain("vector")
            else:
                hq_done = emit_quant(I, NBI, h_sb[:, 0:I], qy[:, 0:I],
                                     qy[:, 0:I], qa[:, 0:I], h_sb[:, 0:I])
            P.wait("gpsimd", "dv", hq_done)
            hq_spilled[ch] = P.dma("gpsimd", "hqsp", hqsp[row0:row0 + 128, :],
                                   qy[:, 0:I])

        P.wait("tensor", "cbfq0", cb_sq)
        prev_out_done = 0
        hq_guard = dn_pe_end[t - 1] if t >= 1 else 0
        for half in range(2):
            f0 = half * 11
            hq_sq = 0
            for c in range(11):
                fc = f0 + c
                P.wait("sync", "hqsp", hq_spilled[t * 4 + 3])
                P.wait("sync", "pe", hq_guard)
                hq_sq = P.dma("sync", "hqfq0", hqfm[:, c * TILE:(c + 1) * TILE],
                              hqsp[t * TILE:(t + 1) * TILE,
                                   fc * 128:(fc + 1) * 128],
                              transpose=True)
            P.wait("tensor", "hqfq0", hq_sq)
            for j in range(4):
                ch = t * 4 + j
                row0 = ch * 128
                if half == 0:
                    P.wait("tensor", "dv", P.val("dv") if j == 0 else 0)
                    for s in range(2):
                        for k in range(8):
                            P.run("tensor", lambda e, j=j, s=s, k=k: e.matmul(
                                bank(2 * j + s),
                                combfm[:, k * TILE + j * 128:
                                       k * TILE + (j + 1) * 128],
                                wout_s2[:, k * D + s * 512:
                                        k * D + (s + 1) * 512],
                                start=(k == 0), stop=False))
                        for c in range(11):
                            P.run("tensor", lambda e, j=j, s=s, c=c,
                                  f0=f0: e.matmul(
                                bank(2 * j + s),
                                hqfm[:, c * TILE + j * 128:
                                     c * TILE + (j + 1) * 128],
                                wdq_fm[:, (f0 + c) * D + s * 512:
                                       (f0 + c) * D + (s + 1) * 512],
                                start=False, stop=False),
                                s="pe" if c == 10 else None)
                else:
                    for s in range(2):
                        for c in range(11):
                            P.run("tensor", lambda e, j=j, s=s, c=c,
                                  f0=f0: e.matmul(
                                bank(2 * j + s),
                                hqfm[:, c * TILE + j * 128:
                                     c * TILE + (j + 1) * 128],
                                wdq_fm[:, (f0 + c) * D + s * 512:
                                       (f0 + c) * D + (s + 1) * 512],
                                start=False, stop=(c == 10)),
                                s="pe" if c == 10 else None)
                    pe_out = P.val("pe")
                    P.wait("gpsimd", "dv", prev_out_done)
                    ldx2 = P.dma("gpsimd", "x2l", x2_sb[:, 0:D],
                                 x_h[row0:row0 + 128, :])
                    P.wait("vector", "pe", pe_out)
                    P.wait("vector", "x2l", ldx2)
                    if ch >= 1:
                        P.wait("vector", "outst", out_stored[ch - 1])
                    out_done = P.run("vector",
                        lambda e, j=j: e.tensor_tensor(
                            out=out_sb[:, 0:D],
                            in0=bank(2 * j, 1024),
                            in1=x2_sb[:, 0:D], op=ALU.add), s="dv")
                    P.drain("vector")
                    prev_out_done = out_done
                    P.wait("scalar", "dv", out_done)
                    out_stored[ch] = P.dma("scalar", "outst",
                                           y_h[row0:row0 + 128, :],
                                           out_sb[:, 0:D])
            hq_guard = P.val("pe")
            dn_pe_end[t] = P.val("pe")
        P.wait("tensor", "dv", prev_out_done)

    P.wait("gpsimd", "outst", out_stored[NC_TOK - 1])
    P.wait("sync", "outst", out_stored[NC_TOK - 1])

    with nc.Block() as block:
        @block.sync
        def _(e):
            for fn in P.ops["sync"]:
                fn(e)

        @block.scalar
        def _(e):
            for fn in P.ops["scalar"]:
                fn(e)

        @block.vector
        def _(e):
            for fn in P.ops["vector"]:
                fn(e)

        @block.tensor
        def _(e):
            for fn in P.ops["tensor"]:
                fn(e)

        @block.gpsimd
        def _(e):
            for fn in P.ops["gpsimd"]:
                fn(e)

    return nc


def kernel(L0_attn_x, L0_attn_wq, L0_attn_wk, L0_attn_wv, L0_attn_wgq,
           L0_attn_wgk, L0_attn_wgv, L0_attn_r_out_w, L0_attn_gate_local_w,
           L0_attn_gate_remote_w, L0_attn_w_out, L0_ffn_w_gate, L0_ffn_w_up,
           L0_ffn_w_down):
    B, S, Dm = L0_attn_x.shape
    ncore = 8
    per = B // ncore
    nc_prog = build_core(nseq=per, seqlen=S)
    mybir.codegen_inst_isa_subclasses(nc_prog)
    shards = [np.ascontiguousarray(
        L0_attn_x[c * per:(c + 1) * per].reshape(per * S, Dm), dtype=np.float32)
        for c in range(ncore)]
    w = dict(
        wq=np.ascontiguousarray(L0_attn_wq, np.float32),
        wk=np.ascontiguousarray(L0_attn_wk, np.float32),
        wv=np.ascontiguousarray(L0_attn_wv, np.float32),
        wgq=np.ascontiguousarray(L0_attn_wgq, np.float32),
        wgk=np.ascontiguousarray(L0_attn_wgk, np.float32),
        wgv=np.ascontiguousarray(L0_attn_wgv, np.float32),
        rout=np.ascontiguousarray(L0_attn_r_out_w, np.float32),
        glw=np.ascontiguousarray(L0_attn_gate_local_w, np.float32),
        grw=np.ascontiguousarray(L0_attn_gate_remote_w, np.float32),
        wout=np.ascontiguousarray(L0_attn_w_out, np.float32),
        wgate=np.ascontiguousarray(L0_ffn_w_gate, np.float32),
        wup=np.ascontiguousarray(L0_ffn_w_up, np.float32),
        wdown=np.ascontiguousarray(L0_ffn_w_down, np.float32),
    )
    in_maps = [dict(w, x=shards[c]) for c in range(ncore)]
    res = run_bass_kernel_spmd(nc_prog, in_maps, core_ids=list(range(ncore)))
    out = np.stack([np.asarray(res.results[c]["y"]).reshape(per, S, Dm)
                    for c in range(ncore)], 0)
    return out.reshape(B, S, Dm).astype(np.float32)
